# version 03
# baseline (speedup 1.0000x reference)
import numpy as np
import ml_dtypes
import concourse.bass as bass
import concourse.mybir as mybir
from concourse.bass_utils import run_bass_kernel_spmd

F32 = mybir.dt.float32
BF16 = mybir.dt.bfloat16
AF = mybir.ActivationFunctionType
ALU = mybir.AluOpType
AX = mybir.AxisListType

D = 1024
DC = 8
S = 4096
B = 4
TOK = 2048
NSLOT = 4
SL = 512
NCORE = 8
POS = [[0, 3, 4, 7], [1, 2, 5, 6]]
LN_EPS = 1e-5

ENGINES = ("tensor", "vector", "scalar", "gpsimd", "sync")


class _Op:
    __slots__ = ("eng", "seq", "fn", "reads", "writes", "is_dma", "needs_inc",
                 "waits", "inc_count", "tag", "dma_slot", "dma_use")

    def __init__(self, eng, seq, fn, reads, writes, is_dma, tag):
        self.eng = eng
        self.seq = seq
        self.fn = fn
        self.reads = reads
        self.writes = writes
        self.is_dma = is_dma
        self.needs_inc = False
        self.waits = {}
        self.inc_count = 0
        self.tag = tag
        self.dma_slot = -1
        self.dma_use = 0


def _rkey(r):
    sub = None
    if isinstance(r, tuple):
        r, sub = r
    h = getattr(r, "tensor", r)
    name = getattr(h, "name", None)
    if name is None:
        name = id(h)
    return (name, sub)


NDMASEM = 8


class Emitter:

    def __init__(self, nc):
        self.nc = nc
        self.ops = []
        self.last_writer = {}
        self.readers = {}
        self.per_eng = {e: [] for e in ENGINES}
        self.alias = {}
        self.dma_count = {e: 0 for e in ENGINES}

    def set_alias(self, tensors, group):
        for t in tensors:
            self.alias[getattr(t, "name", t)] = group

    def _key(self, r):
        name, sub = _rkey(r)
        g = self.alias.get(name)
        if g is not None:
            return (g, None)
        return (name, sub)

    def emit(self, eng, fn, reads=(), writes=(), dma=False, after=(), tag=""):
        op = _Op(eng, len(self.ops), fn, [self._key(r) for r in reads],
                 [self._key(r) for r in writes], dma, tag)
        if dma:
            i = self.dma_count[eng]
            self.dma_count[eng] = i + 1
            op.dma_slot = i % NDMASEM
            op.dma_use = i // NDMASEM + 1
        deps = list(after)
        for r in op.reads:
            w = self.last_writer.get(r)
            if w is not None:
                deps.append(w)
        for r in op.writes:
            w = self.last_writer.get(r)
            if w is not None:
                deps.append(w)
            deps.extend(self.readers.get(r, ()))
        for d in deps:
            if d is op:
                continue
            if d.eng == eng == "tensor" and not d.is_dma and not dma:
                continue
            ch = ("d", d.eng, d.dma_slot) if d.is_dma else ("e", d.eng)
            cur = op.waits.get(ch)
            if cur is None or d.seq > cur.seq:
                op.waits[ch] = d
        for r in op.writes:
            self.last_writer[r] = op
            self.readers[r] = []
        for r in op.reads:
            self.readers.setdefault(r, []).append(op)
        self.ops.append(op)
        self.per_eng[eng].append(op)
        return op

    def build(self, block_name=None):
        nc = self.nc
        for op in self.ops:
            for d in op.waits.values():
                if not d.is_dma:
                    d.needs_inc = True
        sem_final = {}
        for e in ENGINES:
            c = 0
            for op in self.per_eng[e]:
                if op.needs_inc and not op.is_dma:
                    c += 1
                op.inc_count = c
            sem_final[e] = c
        sems = {e: nc.alloc_semaphore(f"em_{e}") for e in ENGINES}
        dsems = {e: [nc.alloc_semaphore(f"dm_{e}{i}") for i in range(NDMASEM)]
                 for e in ENGINES if self.dma_count[e] > 0}

        def _resolve(ch, dop):
            if ch[0] == "d":
                return dsems[ch[1]][ch[2]], 16 * dop.dma_use
            return sems[ch[1]], dop.inc_count

        def runner(e):
            def run(engine):
                clock = {}
                for op in self.per_eng[e]:
                    if op.is_dma and op.dma_use > 1:
                        ch = ("d", e, op.dma_slot)
                        v = 16 * (op.dma_use - 1)
                        if clock.get(ch, 0) < v:
                            engine.wait_ge(dsems[e][op.dma_slot], v)
                            clock[ch] = v
                    for ch in sorted(op.waits, key=str):
                        sem, v = _resolve(ch, op.waits[ch])
                        if v > clock.get(ch, 0):
                            engine.wait_ge(sem, v)
                            clock[ch] = v
                    inst = op.fn(engine)
                    if op.is_dma:
                        inst.then_inc(dsems[e][op.dma_slot], 16)
                    elif op.needs_inc:
                        inst.then_inc(sems[e], 1)
            return run

        with nc.Block(block_name) as blk:
            blk.tensor(runner("tensor"))
            blk.vector(runner("vector"))
            blk.scalar(runner("scalar"))
            blk.gpsimd(runner("gpsimd"))
            blk.sync(runner("sync"))
        return sems, sem_final


def _mk_nc():
    return bass.Bass("TRN2", target_bir_lowering=False)


def _in(nc, name, shape, dtype=F32):
    return nc.dram_tensor(name, shape, dtype, kind="ExternalInput").ap()


def _out(nc, name, shape, dtype=F32):
    return nc.dram_tensor(name, shape, dtype, kind="ExternalOutput").ap()


def load_fm(em, dst, src, chunks, width, eng="sync", tag="ld"):
    ap = src.rearrange("(c p) x -> p c x", p=128)
    dv = dst[:, :].rearrange("p (c x) -> p c x", c=chunks)
    em.emit(eng, lambda e: e.dma_start(out=dv, in_=ap), writes=[dst], dma=True, tag=tag)


def store_fm(em, dst, src_sb, chunks, width, col0=0, eng="sync", tag="st", reads=()):
    ap = dst.rearrange("(c p) x -> p c x", p=128)
    sv = src_sb[:, col0:col0 + chunks * width].rearrange("p (c x) -> p c x", c=chunks)
    return em.emit(eng, lambda e: e.dma_start(out=ap, in_=sv),
                   reads=list(reads) or [src_sb], dma=True, tag=tag)


class Common:

    def __init__(self, nc, em):
        self.nc, self.em = nc, em
        self.ones_mean = nc.alloc_sbuf_tensor("ones_mean", [128, 1], BF16)
        self.ones_row = nc.alloc_sbuf_tensor("ones_row", [1, 128], F32)
        self.ones_rowb = nc.alloc_sbuf_tensor("ones_rowb", [1, 128], BF16)
        self.ones_colb = nc.alloc_sbuf_tensor("ones_colb", [128, 1], BF16)
        em.emit("vector", lambda e: e.memset(self.ones_mean[:, :], 1.0 / D),
                writes=[self.ones_mean])
        em.emit("vector", lambda e: e.memset(self.ones_row[:, :], 1.0),
                writes=[self.ones_row])
        em.emit("vector", lambda e: e.memset(self.ones_rowb[:, :], 1.0),
                writes=[self.ones_rowb])
        em.emit("vector", lambda e: e.memset(self.ones_colb[:, :], 1.0),
                writes=[self.ones_colb])
        self.eps = nc.alloc_sbuf_tensor("eps_c", [1, 1], F32)
        em.emit("vector", lambda e: e.memset(self.eps[:, :], float(LN_EPS)),
                writes=[self.eps])
        self.xrb = self.x2b = self.rows = None
        self.Ab = nc.alloc_sbuf_tensor("ln_Ab", [128, SL], F32)
        self.Bb = nc.alloc_sbuf_tensor("ln_Bb", [128, SL], F32)
        self.tmp = nc.alloc_sbuf_tensor("ln_tmp", [128, SL], F32)

    def attach_ln_scratch(self, xrb, x2b, rows):
        self.xrb, self.x2b, self.rows = xrb, x2b, rows

    def layernorm_nt(self, xres, bk_stat, bk_bcast, out_f32=None, out_bf16=None):
        em = self.em
        R = self.rows
        mrow, erow = R[0:1, 0:SL], R[0:1, SL:2 * SL]
        m2 = var = lnv = R[0:1, 2 * SL:3 * SL]
        rstd = R[0:1, 3 * SL:4 * SL]
        mr = negmr = R[0:1, 2 * SL:3 * SL]
        for fc in range(DC):
            sl = slice(fc * SL, (fc + 1) * SL)
            em.emit("vector", lambda e, sl=sl: e.tensor_copy(self.xrb[:, sl], xres[:, sl]),
                    reads=[xres], writes=[self.xrb])
            em.emit("scalar", lambda e, sl=sl: e.activation(self.x2b[:, sl], xres[:, sl], AF.Square),
                    reads=[xres], writes=[self.x2b])
        for fc in range(DC):
            sl = slice(fc * SL, (fc + 1) * SL)
            em.emit("tensor", lambda e, sl=sl, fc=fc: e.matmul(
                bk_stat[0:1, :], self.ones_mean[:, :], self.xrb[:, sl],
                start=(fc == 0), stop=(fc == DC - 1)),
                reads=[self.xrb, self.ones_mean], writes=[bk_stat])
            em.emit("tensor", lambda e, sl=sl, fc=fc: e.matmul(
                bk_bcast[0:1, :], self.ones_mean[:, :], self.x2b[:, sl],
                start=(fc == 0), stop=(fc == DC - 1)),
                reads=[self.x2b, self.ones_mean], writes=[bk_bcast])
        em.emit("vector", lambda e: e.tensor_copy(mrow, bk_stat[0:1, :]),
                reads=[bk_stat], writes=[R])
        em.emit("vector", lambda e: e.tensor_copy(erow, bk_bcast[0:1, :]),
                reads=[bk_bcast], writes=[R])
        em.emit("vector", lambda e: e.tensor_mul(m2, mrow, mrow), reads=[R], writes=[R])
        em.emit("vector", lambda e: e.tensor_sub(var, erow, m2), reads=[R], writes=[R])
        em.emit("scalar", lambda e: e.activation(lnv, var, AF.Ln, bias=self.eps[0:1, 0:1]),
                reads=[R, self.eps], writes=[R])
        em.emit("scalar", lambda e: e.activation(rstd, lnv, AF.Exp, scale=-0.5),
                reads=[R], writes=[R])
        em.emit("vector", lambda e: e.tensor_mul(mr, mrow, rstd), reads=[R], writes=[R])
        em.emit("vector", lambda e: e.tensor_scalar(negmr, mr, -1.0, None, ALU.mult),
                reads=[R], writes=[R])
        em.emit("tensor", lambda e: e.matmul(bk_stat[:, :], self.ones_row[:, :], rstd,
                                             start=True, stop=True),
                reads=[R, self.ones_row], writes=[bk_stat])
        em.emit("vector", lambda e: e.tensor_copy(self.Ab[:, :], bk_stat[:, :]),
                reads=[bk_stat], writes=[self.Ab])
        em.emit("tensor", lambda e: e.matmul(bk_bcast[:, :], self.ones_row[:, :], negmr,
                                             start=True, stop=True),
                reads=[R, self.ones_row], writes=[bk_bcast])
        em.emit("vector", lambda e: e.tensor_copy(self.Bb[:, :], bk_bcast[:, :]),
                reads=[bk_bcast], writes=[self.Bb])
        for fc in range(DC):
            sl = slice(fc * SL, (fc + 1) * SL)
            em.emit("vector", lambda e, sl=sl: e.tensor_mul(self.tmp[:, :], xres[:, sl], self.Ab[:, :]),
                    reads=[xres, self.Ab], writes=[self.tmp])
            if out_f32 is not None:
                em.emit("vector", lambda e, sl=sl: e.tensor_add(out_f32[:, sl], self.tmp[:, :], self.Bb[:, :]),
                        reads=[self.tmp, self.Bb], writes=[out_f32])
                if out_bf16 is not None:
                    em.emit("scalar", lambda e, sl=sl: e.activation(out_bf16[:, sl], out_f32[:, sl], AF.Copy),
                            reads=[out_f32], writes=[out_bf16])
            elif out_bf16 is not None:
                em.emit("vector", lambda e, sl=sl: e.tensor_add(out_bf16[:, sl], self.tmp[:, :], self.Bb[:, :]),
                        reads=[self.tmp, self.Bb], writes=[out_bf16])


def build_A():
    nc = _mk_nc()
    xseq = _in(nc, "xseq", [D, S])
    WvT = _in(nc, "WvT", [D, 256])
    WuT = _in(nc, "WuT", [256, 3 * D])
    WovT = _in(nc, "WovT", [D, 256])
    WouT = _in(nc, "WouT", [256, D])
    x1 = _out(nc, "x1", [D, TOK])

    em = Emitter(nc)
    cm = Common(nc, em)

    xb = nc.alloc_sbuf_tensor("xb", [128, DC * S], BF16)
    xb_off = nc.lookup_mloc(xb).addr
    xres = nc.alloc_sbuf_tensor_at("xres", [128, DC * SL], F32, offset=xb_off)
    xot = nc.alloc_sbuf_tensor_at("xot", [128, DC * SL], F32, offset=xb_off + 16384)
    xrb = nc.alloc_sbuf_tensor_at("ln_xrb", [128, DC * SL], BF16, offset=xb_off + 32768)
    x2b = nc.alloc_sbuf_tensor_at("ln_x2b", [128, DC * SL], BF16, offset=xb_off + 40960)
    lrows = nc.alloc_sbuf_tensor_at("ln_rows", [1, 4 * SL], F32, offset=xb_off + 49152)
    em.set_alias([xb, xres, xot, xrb, x2b, lrows], "R1")
    cm.attach_ln_scratch(xrb, x2b, lrows)

    wv = nc.alloc_sbuf_tensor("wv", [128, DC * 256], BF16)
    wu = nc.alloc_sbuf_tensor("wu", [128, 2 * 3 * D], BF16)
    wov = nc.alloc_sbuf_tensor("wov", [128, DC * 256], BF16)
    wou = nc.alloc_sbuf_tensor("wou", [128, 2 * D], BF16)
    t = nc.alloc_sbuf_tensor("t", [128, 2 * S], BF16)
    Qg = nc.alloc_sbuf_tensor("Qg", [128, S], BF16)
    Kg = nc.alloc_sbuf_tensor("Kg", [128, TOK], BF16)
    Vg = nc.alloc_sbuf_tensor("Vg", [128, 32 * 130], BF16)
    y = nc.alloc_sbuf_tensor("y", [128, DC * TOK], BF16)
    z1 = nc.alloc_sbuf_tensor("z1", [128, 2 * TOK], BF16)
    Pp = nc.alloc_sbuf_tensor("Pp", [128, 4 * SL], BF16)
    sq = nc.alloc_sbuf_tensor("sq", [128, S], BF16)
    r_sc = nc.alloc_sbuf_tensor("r_sc", [1, 528], F32)
    negc = nc.alloc_sbuf_tensor("negc", [128, 1], F32)
    rl = nc.alloc_sbuf_tensor("rl", [1, SL], F32)
    rlb = nc.alloc_sbuf_tensor("rlb", [64, SL], F32)

    bk = [nc.alloc_psum_tensor(f"bk{i}", [128, 512], F32) for i in range(8)]

    E = em.emit

    load_fm(em, xb, xseq, DC, S, eng="gpsimd")
    load_fm(em, wv, WvT, DC, 256, eng="gpsimd")
    load_fm(em, wu, WuT, 2, 3 * D, eng="gpsimd")
    load_fm(em, wov, WovT, DC, 256, eng="gpsimd")
    load_fm(em, wou, WouT, 2, D, eng="gpsimd")

    for rc in range(2):
        for ntile in range(S // 512):
            pb = bk[ntile % 2]
            for dc in range(DC):
                E("tensor", lambda e, rc=rc, nt=ntile, dc=dc, pb=pb: e.matmul(
                    pb[:, :], wv[:, dc * 256 + rc * 128: dc * 256 + rc * 128 + 128],
                    xb[:, dc * S + nt * 512: dc * S + nt * 512 + 512],
                    start=(dc == 0), stop=(dc == DC - 1)),
                    reads=[wv, xb], writes=[pb])
            E("scalar", lambda e, rc=rc, nt=ntile, pb=pb: e.activation(
                t[:, rc * S + nt * 512: rc * S + nt * 512 + 512], pb[:, :], AF.Copy),
                reads=[pb], writes=[(t, (rc, ntile))])

    for g in range(8):
        fq = g * 128
        fk = D + g * 128
        fv = 2 * D + g * 128
        for ntile in range(S // 512):
            pb = bk[ntile % 2]
            for rc in range(2):
                E("tensor", lambda e, nt=ntile, rc=rc, pb=pb, fq=fq: e.matmul(
                    pb[:, :], wu[:, rc * 3 * D + fq: rc * 3 * D + fq + 128],
                    t[:, rc * S + nt * 512: rc * S + nt * 512 + 512],
                    start=(rc == 0), stop=(rc == 1)),
                    reads=[wu, (t, (rc, ntile))], writes=[pb])
            E("scalar", lambda e, nt=ntile, pb=pb: e.activation(
                Qg[:, nt * 512: nt * 512 + 512], pb[:, :], AF.Copy),
                reads=[pb], writes=[(Qg, ntile)])
        for ntile in range(TOK // 512):
            pb = bk[2 + ntile % 2]
            for rc in range(2):
                E("tensor", lambda e, nt=ntile, rc=rc, pb=pb, fk=fk: e.matmul(
                    pb[:, :], wu[:, rc * 3 * D + fk: rc * 3 * D + fk + 128],
                    t[:, rc * S + nt * 512: rc * S + nt * 512 + 512],
                    start=(rc == 0), stop=(rc == 1)),
                    reads=[wu, (t, (rc, ntile))], writes=[pb])
            E("scalar", lambda e, nt=ntile, pb=pb: e.activation(
                Kg[:, nt * 512: nt * 512 + 512], pb[:, :], AF.Copy),
                reads=[pb], writes=[(Kg, ntile)])
        for jc in range(32):
            pb = bk[4 + jc % 2]
            for rc in range(2):
                E("tensor", lambda e, jc=jc, rc=rc, pb=pb, fv=fv: e.matmul(
                    pb[:, 0:128], t[:, rc * S + jc * 128: rc * S + jc * 128 + 128],
                    wu[:, rc * 3 * D + fv: rc * 3 * D + fv + 128],
                    start=(rc == 0), stop=(rc == 1)),
                    reads=[wu, (t, (rc, jc // 4))], writes=[pb])
            for h in range(2):
                E("vector", lambda e, jc=jc, h=h, pb=pb: e.tensor_copy(
                    Vg[:, jc * 130 + h * 65: jc * 130 + h * 65 + 64],
                    pb[:, h * 64: h * 64 + 64]),
                    reads=[pb], writes=[(Vg, jc)])
        E("vector", lambda e: e.memset(
            Vg[:, :].rearrange("p (j h c) -> p j h c", j=32, h=2)[:, :, :, 64:65], 1.0),
            writes=[(Vg, "ones")])

        scr = r_sc[0:1, 0:512]
        mk2 = r_sc[0:1, 512:513]
        mq2 = r_sc[0:1, 513:514]
        prd = r_sc[0:1, 514:515]
        sqr = r_sc[0:1, 515:516]
        ngc = r_sc[0:1, 516:517]
        cur = r_sc[0:1, 517:518]
        E("vector", lambda e: e.memset(mk2, 0.0), writes=[(r_sc, "mk2")])
        E("vector", lambda e: e.memset(mq2, 0.0), writes=[(r_sc, "mq2")])
        for src_, nmax, mrow, mkey in ((Kg, TOK // 512, mk2, "mk2"), (Qg, S // 512, mq2, "mq2")):
            for ntile in range(nmax):
                E("scalar", lambda e, nt=ntile, src_=src_: e.activation(
                    sq[:, nt * 512: nt * 512 + 512], src_[:, nt * 512: nt * 512 + 512], AF.Square),
                    reads=[(src_, ntile)], writes=[(sq, ntile)])
            for ntile in range(nmax):
                pb = bk[6]
                E("tensor", lambda e, nt=ntile, pb=pb: e.matmul(
                    pb[0:1, :], cm.ones_colb[:, :], sq[:, nt * 512: nt * 512 + 512],
                    start=True, stop=True),
                    reads=[(sq, ntile), cm.ones_colb], writes=[pb])
                E("vector", lambda e, pb=pb: e.tensor_copy(scr, pb[0:1, :]),
                    reads=[pb], writes=[(r_sc, "scr")])
                E("vector", lambda e: e.reduce_max(cur, scr, axis=AX.X),
                    reads=[(r_sc, "scr")], writes=[(r_sc, "cur")])
                E("vector", lambda e, mrow=mrow, mkey=mkey: e.tensor_max(mrow, mrow, cur),
                    reads=[(r_sc, "cur"), (r_sc, mkey)], writes=[(r_sc, mkey)])
        E("vector", lambda e: e.tensor_mul(prd, mk2, mq2),
          reads=[(r_sc, "mk2"), (r_sc, "mq2")], writes=[(r_sc, "prd")])
        E("scalar", lambda e: e.activation(sqr, prd, AF.Sqrt),
          reads=[(r_sc, "prd")], writes=[(r_sc, "sqr")])
        E("vector", lambda e: e.tensor_scalar(ngc, sqr, -0.125, None, ALU.mult),
          reads=[(r_sc, "sqr")], writes=[(r_sc, "ngc")])
        pb = bk[6]
        E("tensor", lambda e, pb=pb: e.matmul(pb[:, 0:1], cm.ones_row[:, :], ngc,
                                              start=True, stop=True),
          reads=[(r_sc, "ngc"), cm.ones_row], writes=[pb])
        E("vector", lambda e, pb=pb: e.tensor_copy(negc[:, :], pb[:, 0:1]),
          reads=[pb], writes=[negc])

        for it in range(NSLOT):
            nb = [bk[4], bk[5]]
            for h in range(2):
                pbase = h * 64
                nbank = nb[h]
                for jc in range(32):
                    sb_ = bk[2 * h + (jc % 2)]
                    E("tensor", lambda e, jc=jc, it=it, pbase=pbase, sb_=sb_: e.matmul(
                        sb_[:, :], Qg[pbase:pbase + 64, jc * 128: jc * 128 + 128],
                        Kg[pbase:pbase + 64, it * 512: it * 512 + 512],
                        start=True, stop=True),
                        reads=[(Qg, jc // 4), (Kg, it)], writes=[sb_])
                    psl = slice((2 * h + (jc % 2)) * SL, (2 * h + (jc % 2)) * SL + SL)
                    E("scalar", lambda e, psl=psl, sb_=sb_: e.activation(
                        Pp[:, psl], sb_[:, :], AF.Exp, bias=negc[:, 0:1], scale=0.125),
                        reads=[sb_, negc], writes=[(Pp, psl.start)])
                    E("tensor", lambda e, jc=jc, h=h, psl=psl, nbank=nbank: e.matmul(
                        nbank[0:65, :], Vg[:, jc * 130 + h * 65: jc * 130 + h * 65 + 65],
                        Pp[:, psl], start=(jc == 0), stop=(jc == 31)),
                        reads=[(Vg, jc), (Vg, "ones"), (Pp, psl.start)], writes=[nbank])
            for h in range(2):
                nbank = nb[h]
                E("vector", lambda e, nbank=nbank: e.reciprocal(rl[0:1, :], nbank[64:65, :]),
                  reads=[nbank], writes=[rl])
                pb2 = bk[6]
                E("tensor", lambda e, pb2=pb2: e.matmul(pb2[0:64, :], cm.ones_row[0:1, 0:64], rl[0:1, :],
                                                        start=True, stop=True),
                  reads=[rl, cm.ones_row], writes=[pb2])
                E("vector", lambda e, pb2=pb2: e.tensor_copy(rlb[:, :], pb2[0:64, :]),
                  reads=[pb2], writes=[rlb])
                hh = 2 * g + h
                fc_, pb_ = hh // 2, (hh % 2) * 64
                E("vector", lambda e, it=it, fc_=fc_, pb_=pb_, nbank=nbank: e.tensor_mul(
                    y[pb_:pb_ + 64, fc_ * TOK + it * 512: fc_ * TOK + it * 512 + 512],
                    nbank[0:64, :], rlb[:, :]),
                    reads=[nbank, rlb], writes=[(y, (fc_, pb_, it))])

    for rc in range(2):
        for ntile in range(NSLOT):
            pb = bk[ntile % 2]
            for dc in range(DC):
                E("tensor", lambda e, rc=rc, nt=ntile, dc=dc, pb=pb: e.matmul(
                    pb[:, :], wov[:, dc * 256 + rc * 128: dc * 256 + rc * 128 + 128],
                    y[:, dc * TOK + nt * 512: dc * TOK + nt * 512 + 512],
                    start=(dc == 0), stop=(dc == DC - 1)),
                    reads=[wov, (y, (dc, 0, ntile)), (y, (dc, 64, ntile))], writes=[pb])
            E("scalar", lambda e, rc=rc, nt=ntile, pb=pb: e.activation(
                z1[:, rc * TOK + nt * 512: rc * TOK + nt * 512 + 512], pb[:, :], AF.Copy),
                reads=[pb], writes=[(z1, (rc, ntile))])

    for ntile in range(NSLOT):
        xsl = xseq.rearrange("(c p) x -> p c x", p=128)[:, :, ntile * 512:(ntile + 1) * 512]
        E("sync", lambda e, xsl=xsl: e.dma_start(
            out=xot[:, :].rearrange("p (c x) -> p c x", c=DC), in_=xsl),
            writes=[xot], dma=True)
        for fc in range(DC):
            pb = bk[2 + fc % 2]
            for rc in range(2):
                E("tensor", lambda e, fc=fc, nt=ntile, rc=rc, pb=pb: e.matmul(
                    pb[:, :], wou[:, rc * D + fc * 128: rc * D + fc * 128 + 128],
                    z1[:, rc * TOK + nt * 512: rc * TOK + nt * 512 + 512],
                    start=(rc == 0), stop=(rc == 1)),
                    reads=[wou, (z1, (rc, ntile))], writes=[pb])
            E("vector", lambda e, fc=fc, pb=pb: e.tensor_add(
                xres[:, fc * SL:(fc + 1) * SL], pb[:, :], xot[:, fc * SL:(fc + 1) * SL]),
                reads=[pb, xot], writes=[xres])
        cm.layernorm_nt(xres, bk[6], bk[7], out_f32=xres)
        dstv = x1.rearrange("(c p) x -> p c x", p=128)[:, :, ntile * 512:(ntile + 1) * 512]
        srcv = xres[:, :].rearrange("p (c x) -> p c x", c=DC)
        E("sync", lambda e, dstv=dstv, srcv=srcv: e.dma_start(out=dstv, in_=srcv),
          reads=[xres], dma=True, tag="out")

    outs = [op for op in em.ops if op.tag == "out"]
    em.emit("sync", lambda e: e.nop(), after=outs, tag="final")
    em.build("A")
    return nc


def build_B():
    nc = _mk_nc()
    x1 = _in(nc, "x1", [D, TOK])
    w1T = _in(nc, "w1T", [D, 4096])
    w2T = _in(nc, "w2T", [4096, D])
    b1c = _in(nc, "b1c", [128, 32])
    b2c = _in(nc, "b2c", [128, 8])
    x2 = _out(nc, "x2", [D, TOK])
    s1s = nc.dram_tensor("s1s", [32, 128, 1024], BF16, kind="Internal").ap()
    s2s = nc.dram_tensor("s2s", [4096, D], BF16, kind="Internal").ap()

    em = Emitter(nc)
    cm = Common(nc, em)
    xrb = nc.alloc_sbuf_tensor("ln_xrb", [128, DC * SL], BF16)
    x2b = nc.alloc_sbuf_tensor("ln_x2b", [128, DC * SL], BF16)
    lrows = nc.alloc_sbuf_tensor("ln_rows", [1, 4 * SL], F32)
    cm.attach_ln_scratch(xrb, x2b, lrows)

    xsb = nc.alloc_sbuf_tensor("xsb", [128, DC * TOK], BF16)
    h = nc.alloc_sbuf_tensor("h", [128, 32 * SL], BF16)
    xot = nc.alloc_sbuf_tensor("xot", [128, DC * SL], F32)
    xres = nc.alloc_sbuf_tensor("xres", [128, DC * SL], F32)
    wblk = nc.alloc_sbuf_tensor("wblk", [128, 1024], F32)
    sgn = nc.alloc_sbuf_tensor("sgn", [128, 1024], BF16)
    ab = nc.alloc_sbuf_tensor("ab", [128, 1024], BF16)
    s1buf = [nc.alloc_sbuf_tensor(f"s1b{i}", [128, 1024], BF16) for i in range(2)]
    s2buf = [nc.alloc_sbuf_tensor(f"s2b{i}", [128, 1024], BF16) for i in range(2)]
    a1c = nc.alloc_sbuf_tensor("a1c", [128, 32], F32)
    a2c = nc.alloc_sbuf_tensor("a2c", [128, 8], F32)
    b1s = nc.alloc_sbuf_tensor("b1s", [128, 32], F32)
    b2s = nc.alloc_sbuf_tensor("b2s", [128, 8], F32)
    ot = nc.alloc_sbuf_tensor("ot", [128, SL], F32)
    ones4k = nc.alloc_sbuf_tensor("ones4k", [128, 1], BF16)
    bk = [nc.alloc_psum_tensor(f"bk{i}", [128, 512], F32) for i in range(8)]
    E = em.emit

    E("vector", lambda e: e.memset(ones4k[:, :], 1.0 / 4096.0), writes=[ones4k])
    E("sync", lambda e: e.dma_start(out=b1s[:, :], in_=b1c), writes=[b1s], dma=True)
    E("sync", lambda e: e.dma_start(out=b2s[:, :], in_=b2c), writes=[b2s], dma=True)
    load_fm(em, xsb, x1, DC, TOK, eng="gpsimd")

    for hfc in range(32):
        wv_ = w1T.rearrange("(dc p) c -> p dc c", p=128)[:, :, hfc * 128:(hfc + 1) * 128]
        E("sync", lambda e, wv_=wv_: e.dma_start(
            out=wblk[:, :].rearrange("p (dc c) -> p dc c", dc=8), in_=wv_),
            writes=[wblk], dma=True)
        E("scalar", lambda e: e.activation(sgn[:, :], wblk[:, :], AF.Sign),
          reads=[wblk], writes=[sgn])
        E("scalar", lambda e: e.activation(ab[:, :], wblk[:, :], AF.Abs),
          reads=[wblk], writes=[ab])
        E("gpsimd", lambda e, hfc=hfc: e.dma_start(out=s1s[hfc], in_=sgn[:, :]),
          reads=[sgn], writes=[(s1s, hfc)], dma=True)
        pb = bk[0]
        for dc in range(DC):
            E("tensor", lambda e, dc=dc, pb=pb: e.matmul(
                pb[:, 0:1], ab[:, dc * 128:(dc + 1) * 128], cm.ones_mean[:, :],
                start=(dc == 0), stop=(dc == DC - 1)),
                reads=[ab, cm.ones_mean], writes=[pb])
        E("vector", lambda e, hfc=hfc, pb=pb: e.tensor_copy(a1c[:, hfc:hfc + 1], pb[:, 0:1]),
          reads=[pb], writes=[a1c])

    pa = bk[1]
    E("vector", lambda e: e.memset(a2c[:, :], 0.0), writes=[a2c])
    for hfc in range(32):
        wv_ = w2T[hfc * 128:(hfc + 1) * 128, :]
        E("sync", lambda e, wv_=wv_: e.dma_start(out=wblk[:, :], in_=wv_),
          writes=[wblk], dma=True)
        E("scalar", lambda e: e.activation(sgn[:, :], wblk[:, :], AF.Sign),
          reads=[wblk], writes=[sgn])
        E("scalar", lambda e: e.activation(ab[:, :], wblk[:, :], AF.Abs),
          reads=[wblk], writes=[ab])
        E("gpsimd", lambda e, hfc=hfc: e.dma_start(out=s2s[hfc * 128:(hfc + 1) * 128, :], in_=sgn[:, :]),
          reads=[sgn], writes=[(s2s, hfc)], dma=True)
        for fc in range(DC):
            E("tensor", lambda e, fc=fc, pa=pa: e.matmul(
                pa[:, fc:fc + 1], ab[:, fc * 128:(fc + 1) * 128], ones4k[:, :],
                start=True, stop=True),
                reads=[ab, ones4k], writes=[pa])
        E("vector", lambda e, pa=pa: e.tensor_add(a2c[:, :], a2c[:, :], pa[:, 0:8]),
          reads=[pa, a2c], writes=[a2c])

    for st in range(4):
        xsl = x1.rearrange("(c p) x -> p c x", p=128)[:, :, st * 512:(st + 1) * 512]
        E("sync", lambda e, xsl=xsl: e.dma_start(
            out=xot[:, :].rearrange("p (c x) -> p c x", c=DC), in_=xsl),
            writes=[xot], dma=True)
        for hfc in range(32):
            sb_ = s1buf[hfc % 2]
            E("sync", lambda e, hfc=hfc, sb_=sb_: e.dma_start(out=sb_[:, :], in_=s1s[hfc]),
              reads=[(s1s, hfc)], writes=[sb_], dma=True)
            pb = bk[hfc % 2]
            for dc in range(DC):
                E("tensor", lambda e, dc=dc, st=st, sb_=sb_, pb=pb: e.matmul(
                    pb[:, :], sb_[:, dc * 128:(dc + 1) * 128],
                    xsb[:, dc * TOK + st * 512: dc * TOK + st * 512 + 512],
                    start=(dc == 0), stop=(dc == DC - 1)),
                    reads=[sb_, xsb], writes=[pb])
            E("scalar", lambda e, hfc=hfc, pb=pb: e.activation(
                h[:, hfc * SL:(hfc + 1) * SL], pb[:, :], AF.Gelu,
                bias=b1s[:, hfc:hfc + 1], scale=a1c[:, hfc:hfc + 1]),
                reads=[pb, b1s, a1c], writes=[(h, hfc)])
        for hfc in range(32):
            sb_ = s2buf[hfc % 2]
            E("sync", lambda e, hfc=hfc, sb_=sb_: e.dma_start(
                out=sb_[:, :], in_=s2s[hfc * 128:(hfc + 1) * 128, :]),
                reads=[(s2s, hfc)], writes=[sb_], dma=True)
            for fc in range(DC):
                E("tensor", lambda e, fc=fc, hfc=hfc, sb_=sb_: e.matmul(
                    bk[fc][:, :], sb_[:, fc * 128:(fc + 1) * 128],
                    h[:, hfc * SL:(hfc + 1) * SL],
                    start=(hfc == 0), stop=(hfc == 31)),
                    reads=[sb_, (h, hfc)], writes=[bk[fc]])
        for fc in range(DC):
            E("vector", lambda e, fc=fc: e.tensor_scalar(
                ot[:, :], bk[fc][:, :], a2c[:, fc:fc + 1], b2s[:, fc:fc + 1],
                ALU.mult, ALU.add),
                reads=[bk[fc], b2s, a2c], writes=[ot])
            E("vector", lambda e, fc=fc: e.tensor_add(
                xres[:, fc * SL:(fc + 1) * SL], ot[:, :], xot[:, fc * SL:(fc + 1) * SL]),
                reads=[ot, xot], writes=[xres])
        cm.layernorm_nt(xres, bk[0], bk[1], out_f32=xres)
        dstv = x2.rearrange("(c p) x -> p c x", p=128)[:, :, st * 512:(st + 1) * 512]
        srcv = xres[:, :].rearrange("p (c x) -> p c x", c=DC)
        E("sync", lambda e, dstv=dstv, srcv=srcv: e.dma_start(out=dstv, in_=srcv),
          reads=[xres], dma=True, tag="out")

    outs = [op for op in em.ops if op.tag == "out"]
    em.emit("sync", lambda e: e.nop(), after=outs, tag="final")
    em.build("B")
    return nc


GPOS = POS[0] + POS[1]
INC_TILES = [list(range(0, it + 1)) + list(range(4, 4 + it + 1)) for it in range(4)]
MSK_TILES = [[it, 4 + it] for it in range(4)]


def build_C1():
    nc = _mk_nc()
    x2 = _in(nc, "x2", [D, TOK])
    wqT = _in(nc, "wqT", [D, D])
    wkT = _in(nc, "wkT", [D, D])
    wvT = _in(nc, "wvT", [D, D])
    Qo = _out(nc, "Qo", [D, TOK], BF16)
    Ko = _out(nc, "Ko", [D, TOK], BF16)
    Vo = _out(nc, "Vo", [TOK, D], BF16)
    q2o = _out(nc, "q2o", [1, TOK])

    em = Emitter(nc)
    cm = Common(nc, em)
    xsb = nc.alloc_sbuf_tensor("xsb", [128, DC * TOK], BF16)
    wT = nc.alloc_sbuf_tensor("wT", [128, DC * D], BF16)
    qt = nc.alloc_sbuf_tensor("qt", [128, DC * SL], BF16)
    sqv = nc.alloc_sbuf_tensor("sqv", [128, SL], BF16)
    q2r = nc.alloc_sbuf_tensor("q2r", [1, SL], F32)
    bk = [nc.alloc_psum_tensor(f"bk{i}", [128, 512], F32) for i in range(4)]
    E = em.emit

    load_fm(em, xsb, x2, DC, TOK, eng="gpsimd")

    for wsrc, dst, with_q2 in ((wqT, Qo, True), (wkT, Ko, False)):
        load_fm(em, wT, wsrc, DC, D, eng="gpsimd", tag="ldw")
        for ntile in range(NSLOT):
            for fc in range(DC):
                pb = bk[fc % 2]
                for dc in range(DC):
                    E("tensor", lambda e, fc=fc, nt=ntile, dc=dc, pb=pb: e.matmul(
                        pb[:, :], wT[:, dc * D + fc * 128: dc * D + fc * 128 + 128],
                        xsb[:, dc * TOK + nt * 512: dc * TOK + nt * 512 + 512],
                        start=(dc == 0), stop=(dc == DC - 1)),
                        reads=[wT, xsb], writes=[pb])
                E("scalar", lambda e, fc=fc, pb=pb: e.activation(
                    qt[:, fc * SL:(fc + 1) * SL], pb[:, :], AF.Copy),
                    reads=[pb], writes=[(qt, fc)])
            dstv = dst.rearrange("(c p) x -> p c x", p=128)[:, :, ntile * 512:(ntile + 1) * 512]
            E("sync", lambda e, dstv=dstv: e.dma_start(
                out=dstv, in_=qt[:, :].rearrange("p (c x) -> p c x", c=DC)),
                reads=[(qt, fc_) for fc_ in range(DC)], dma=True, tag="out")
            if with_q2:
                pq = bk[2]
                for fc in range(DC):
                    E("scalar", lambda e, fc=fc: e.activation(
                        sqv[:, :], qt[:, fc * SL:(fc + 1) * SL], AF.Square),
                        reads=[(qt, fc)], writes=[sqv])
                    E("tensor", lambda e, fc=fc, pq=pq: e.matmul(
                        pq[0:1, :], cm.ones_colb[:, :], sqv[:, :],
                        start=(fc == 0), stop=(fc == DC - 1)),
                        reads=[sqv, cm.ones_colb], writes=[pq])
                E("vector", lambda e, pq=pq: e.tensor_copy(q2r[0:1, :], pq[0:1, :]),
                  reads=[pq], writes=[q2r])
                E("sync", lambda e, nt=ntile: e.dma_start(
                    out=q2o[0:1, nt * 512:(nt + 1) * 512], in_=q2r[0:1, :]),
                    reads=[q2r], dma=True, tag="out")

    load_fm(em, wT, wvT, DC, D, eng="gpsimd", tag="ldw")
    for tc in range(TOK // 128):
        for half in range(2):
            pb = bk[half]
            for dc in range(DC):
                E("tensor", lambda e, tc=tc, half=half, dc=dc, pb=pb: e.matmul(
                    pb[:, :], xsb[:, dc * TOK + tc * 128: dc * TOK + tc * 128 + 128],
                    wT[:, dc * D + half * 512: dc * D + half * 512 + 512],
                    start=(dc == 0), stop=(dc == DC - 1)),
                    reads=[wT, xsb], writes=[pb])
            E("scalar", lambda e, half=half, pb=pb: e.activation(
                qt[:, half * 512: half * 512 + 512], pb[:, :], AF.Copy),
                reads=[pb], writes=[(qt, half)])
        E("sync", lambda e, tc=tc: e.dma_start(
            out=Vo[tc * 128:(tc + 1) * 128, :], in_=qt[:, 0:D]),
            reads=[(qt, 0), (qt, 1)], dma=True, tag="out")

    outs = [op for op in em.ops if op.tag == "out"]
    em.emit("sync", lambda e: e.nop(), after=outs, tag="final")
    em.build("C1")
    return nc


def build_C2():
    nc = _mk_nc()
    x2 = _in(nc, "x2", [D, TOK])
    KoI = _in(nc, "Ko", [D, TOK], BF16)
    QfI = _in(nc, "Qf", [D, S], BF16)
    VfI = _in(nc, "Vf", [S, D], BF16)
    q2f = _in(nc, "q2f", [1, S])
    cmask = _in(nc, "cmask", [4, 8, 128, 512], BF16)
    x3 = _out(nc, "x3", [D, TOK])

    em = Emitter(nc)
    cm = Common(nc, em)
    xrb = nc.alloc_sbuf_tensor("ln_xrb", [128, DC * SL], BF16)
    x2b_ = nc.alloc_sbuf_tensor("ln_x2b", [128, DC * SL], BF16)
    lrows = nc.alloc_sbuf_tensor("ln_rows", [1, 4 * SL], F32)
    cm.attach_ln_scratch(xrb, x2b_, lrows)

    Qf = nc.alloc_sbuf_tensor("Qf_sb", [128, DC * S], BF16)
    Ko = nc.alloc_sbuf_tensor("Ko_sb", [128, DC * TOK], BF16)
    Pp = nc.alloc_sbuf_tensor("Pp", [128, 32 * SL], BF16)
    vbuf = [nc.alloc_sbuf_tensor(f"vb{i}", [128, D], BF16) for i in range(2)]
    mbuf = [nc.alloc_sbuf_tensor(f"mb{i}", [128, SL], BF16) for i in range(2)]
    xot = nc.alloc_sbuf_tensor("xot", [128, DC * SL], F32)
    xres = nc.alloc_sbuf_tensor("xres", [128, DC * SL], F32)
    sqv = nc.alloc_sbuf_tensor("sqv", [128, SL], BF16)
    scr = nc.alloc_sbuf_tensor("scr", [1, SL], F32)
    sca = nc.alloc_sbuf_tensor("sca", [1, 16], F32)
    negc = nc.alloc_sbuf_tensor("negc", [128, 1], F32)
    rl = nc.alloc_sbuf_tensor("rl", [1, SL], F32)
    rlb = nc.alloc_sbuf_tensor("rlb", [128, SL], F32)
    bk = [nc.alloc_psum_tensor(f"bk{i}", [128, 512], F32) for i in range(8)]
    E = em.emit

    load_fm(em, Qf, QfI, DC, S, eng="sync")
    load_fm(em, Ko, KoI, DC, TOK, eng="sync")

    mk2, mq2 = sca[0:1, 0:1], sca[0:1, 1:2]
    prd, sqr = sca[0:1, 2:3], sca[0:1, 3:4]
    ngc, cur = sca[0:1, 4:5], sca[0:1, 5:6]
    E("vector", lambda e: e.memset(mk2, 0.0), writes=[(sca, "mk2")])
    E("vector", lambda e: e.memset(mq2, 0.0), writes=[(sca, "mq2")])
    for ntile in range(NSLOT):
        pq = bk[0]
        for fc in range(DC):
            E("scalar", lambda e, fc=fc, nt=ntile: e.activation(
                sqv[:, :], Ko[:, fc * TOK + nt * 512: fc * TOK + nt * 512 + 512], AF.Square),
                reads=[Ko], writes=[sqv])
            E("tensor", lambda e, fc=fc, pq=pq: e.matmul(
                pq[0:1, :], cm.ones_colb[:, :], sqv[:, :],
                start=(fc == 0), stop=(fc == DC - 1)),
                reads=[sqv, cm.ones_colb], writes=[pq])
        E("vector", lambda e, pq=pq: e.tensor_copy(scr[0:1, :], pq[0:1, :]),
          reads=[pq], writes=[scr])
        E("vector", lambda e: e.reduce_max(cur, scr[0:1, :], axis=AX.X),
          reads=[scr], writes=[(sca, "cur")])
        E("vector", lambda e: e.tensor_max(mk2, mk2, cur),
          reads=[(sca, "cur"), (sca, "mk2")], writes=[(sca, "mk2")])
    for ntile in range(S // 512):
        E("sync", lambda e, nt=ntile: e.dma_start(
            out=scr[0:1, :], in_=q2f[0:1, nt * 512:(nt + 1) * 512]),
            writes=[scr], dma=True)
        E("vector", lambda e: e.reduce_max(cur, scr[0:1, :], axis=AX.X),
          reads=[scr], writes=[(sca, "cur")])
        E("vector", lambda e: e.tensor_max(mq2, mq2, cur),
          reads=[(sca, "cur"), (sca, "mq2")], writes=[(sca, "mq2")])
    E("vector", lambda e: e.tensor_mul(prd, mk2, mq2),
      reads=[(sca, "mk2"), (sca, "mq2")], writes=[(sca, "prd")])
    E("scalar", lambda e: e.activation(sqr, prd, AF.Sqrt),
      reads=[(sca, "prd")], writes=[(sca, "sqr")])
    E("vector", lambda e: e.tensor_scalar(ngc, sqr, -1.0 / 32.0, None, ALU.mult),
      reads=[(sca, "sqr")], writes=[(sca, "ngc")])
    E("tensor", lambda e: e.matmul(bk[3][:, 0:1], cm.ones_row[:, :], ngc, start=True, stop=True),
      reads=[(sca, "ngc"), cm.ones_row], writes=[bk[3]])
    E("vector", lambda e: e.tensor_copy(negc[:, :], bk[3][:, 0:1]),
      reads=[bk[3]], writes=[negc])

    for it in range(NSLOT):
        for jc in range(32):
            sb_ = bk[jc % 2]
            for dc in range(DC):
                E("tensor", lambda e, jc=jc, it=it, dc=dc, sb_=sb_: e.matmul(
                    sb_[:, :], Qf[:, dc * S + jc * 128: dc * S + jc * 128 + 128],
                    Ko[:, dc * TOK + it * 512: dc * TOK + it * 512 + 512],
                    start=(dc == 0), stop=(dc == DC - 1)),
                    reads=[Qf, Ko], writes=[sb_])
            E("scalar", lambda e, jc=jc, sb_=sb_: e.activation(
                Pp[:, jc * SL:(jc + 1) * SL], sb_[:, :], AF.Exp,
                bias=negc[:, 0:1], scale=1.0 / 32.0),
                reads=[sb_, negc], writes=[(Pp, jc)])
            E("tensor", lambda e, jc=jc: e.matmul(
                bk[2][0:1, :], cm.ones_colb[:, :], Pp[:, jc * SL:(jc + 1) * SL],
                start=(jc == 0), stop=(jc == 31)),
                reads=[(Pp, jc), cm.ones_colb], writes=[bk[2]])
        for mi, g in enumerate(MSK_TILES[it]):
            for cl in range(4):
                jc = g * 4 + cl
                mb_ = mbuf[(mi * 4 + cl) % 2]
                E("sync", lambda e, it=it, mi=mi, cl=cl, mb_=mb_: e.dma_start(
                    out=mb_[:, :], in_=cmask[it, mi * 4 + cl]),
                    writes=[mb_], dma=True)
                E("vector", lambda e, jc=jc, mb_=mb_: e.tensor_mul(
                    Pp[:, jc * SL:(jc + 1) * SL], Pp[:, jc * SL:(jc + 1) * SL], mb_[:, :]),
                    reads=[(Pp, jc), mb_], writes=[(Pp, jc)])
        E("vector", lambda e: e.tensor_copy(scr[0:1, :], bk[2][0:1, :]),
          reads=[bk[2]], writes=[scr])
        E("vector", lambda e: e.reciprocal(rl[0:1, :], scr[0:1, :]),
          reads=[scr], writes=[rl])
        E("tensor", lambda e: e.matmul(bk[3][:, :], cm.ones_row[:, :], rl[0:1, :],
                                       start=True, stop=True),
          reads=[rl, cm.ones_row], writes=[bk[3]])
        E("vector", lambda e: e.tensor_copy(rlb[:, :], bk[3][:, :]),
          reads=[bk[3]], writes=[rlb])
        xsl = x2.rearrange("(c p) x -> p c x", p=128)[:, :, it * 512:(it + 1) * 512]
        E("sync", lambda e, xsl=xsl: e.dma_start(
            out=xot[:, :].rearrange("p (c x) -> p c x", c=DC), in_=xsl),
            writes=[xot], dma=True)
        chunks = [g * 4 + cl for g in INC_TILES[it] for cl in range(4)]
        for half in range(2):
            for ji, jc in enumerate(chunks):
                vb_ = vbuf[ji % 2]
                E("sync", lambda e, jc=jc, vb_=vb_: e.dma_start(
                    out=vb_[:, :], in_=VfI[jc * 128:(jc + 1) * 128, :]),
                    writes=[vb_], dma=True)
                for dvl in range(4):
                    dv = half * 4 + dvl
                    E("tensor", lambda e, jc=jc, dv=dv, dvl=dvl, vb_=vb_, ji=ji, n=len(chunks): e.matmul(
                        bk[4 + dvl][:, :], vb_[:, dv * 128:(dv + 1) * 128],
                        Pp[:, jc * SL:(jc + 1) * SL],
                        start=(ji == 0), stop=(ji == n - 1)),
                        reads=[vb_, (Pp, jc)], writes=[bk[4 + dvl]])
            for dvl in range(4):
                dv = half * 4 + dvl
                E("vector", lambda e, dvl=dvl: e.tensor_mul(
                    cm.tmp[:, :], bk[4 + dvl][:, :], rlb[:, :]),
                    reads=[bk[4 + dvl], rlb], writes=[cm.tmp])
                E("vector", lambda e, dv=dv: e.tensor_add(
                    xres[:, dv * SL:(dv + 1) * SL], cm.tmp[:, :], xot[:, dv * SL:(dv + 1) * SL]),
                    reads=[cm.tmp, xot], writes=[xres])
        cm.layernorm_nt(xres, bk[0], bk[1], out_f32=xres)
        dstv = x3.rearrange("(c p) x -> p c x", p=128)[:, :, it * 512:(it + 1) * 512]
        srcv = xres[:, :].rearrange("p (c x) -> p c x", c=DC)
        E("sync", lambda e, dstv=dstv, srcv=srcv: e.dma_start(out=dstv, in_=srcv),
          reads=[xres], dma=True, tag="out")

    outs = [op for op in em.ops if op.tag == "out"]
    em.emit("sync", lambda e: e.nop(), after=outs, tag="final")
    em.build("C2")
    return nc


def build_D():
    nc = _mk_nc()
    x3 = _in(nc, "x3", [D, TOK])
    WgT = _in(nc, "WgT", [D, 8])
    lnsI = _in(nc, "lns", [D, 8])
    lnbI = _in(nc, "lnb", [D, 8])
    W1f = _in(nc, "W1f", [D, 8 * 256])
    b1f = _in(nc, "b1f", [256, 8])
    W2f = _in(nc, "W2f", [256, 8 * D])
    b2I = _in(nc, "b2", [8, D])
    idI = _in(nc, "id128", [128, 128])
    indI = _in(nc, "ind", [8, 1024], BF16)
    xout = _out(nc, "xout", [D, TOK])

    em = Emitter(nc)
    cm = Common(nc, em)
    xrb = nc.alloc_sbuf_tensor("ln_xrb", [128, DC * SL], BF16)
    x2b_ = nc.alloc_sbuf_tensor("ln_x2b", [128, DC * SL], BF16)
    lrows = nc.alloc_sbuf_tensor("ln_rows", [1, 4 * SL], F32)
    cm.attach_ln_scratch(xrb, x2b_, lrows)

    w1e = nc.alloc_sbuf_tensor("w1e", [128, DC * 8 * 256], BF16)
    w2e = nc.alloc_sbuf_tensor("w2e", [128, 2 * 8 * D], BF16)
    wg = nc.alloc_sbuf_tensor("wg", [128, DC * 8], F32)
    lns = nc.alloc_sbuf_tensor("lns_sb", [128, DC * 8], F32)
    lnb = nc.alloc_sbuf_tensor("lnb_sb", [128, DC * 8], F32)
    b1s = nc.alloc_sbuf_tensor("b1s", [128, 2 * 8], F32)
    b2b = nc.alloc_sbuf_tensor("b2b", [8, D], BF16)
    idt = nc.alloc_sbuf_tensor("idt", [128, 128], F32)
    indb = nc.alloc_sbuf_tensor("indb", [8, 1024], BF16)
    xot = nc.alloc_sbuf_tensor("xot", [128, DC * SL], F32)
    xtb = nc.alloc_sbuf_tensor("xtb", [128, DC * SL], BF16)
    xhat = nc.alloc_sbuf_tensor("xhat", [128, DC * SL], BF16)
    xeb = nc.alloc_sbuf_tensor("xeb", [128, DC * SL], BF16)
    hall = nc.alloc_sbuf_tensor("hall", [128, 16 * SL], BF16)
    xres = nc.alloc_sbuf_tensor("xres", [128, DC * SL], F32)
    rmx = nc.alloc_sbuf_tensor("rmx", [128, 1], F32)
    mtm = nc.alloc_sbuf_tensor("mtm", [128, 8], F32)
    maskb = nc.alloc_sbuf_tensor("maskb", [8, SL], BF16)
    bk = [nc.alloc_psum_tensor(f"bk{i}", [128, 512], F32) for i in range(8)]
    E = em.emit

    load_fm(em, wg, WgT, DC, 8, eng="sync")
    load_fm(em, lns, lnsI, DC, 8, eng="sync")
    load_fm(em, lnb, lnbI, DC, 8, eng="sync")
    load_fm(em, w1e, W1f, DC, 8 * 256, eng="gpsimd")
    load_fm(em, b1s, b1f, 2, 8, eng="sync")
    load_fm(em, w2e, W2f, 2, 8 * D, eng="gpsimd")
    E("gpsimd", lambda e: e.dma_start(out=b2b[:, :], in_=b2I), writes=[b2b], dma=True)
    E("sync", lambda e: e.dma_start(out=idt[:, :], in_=idI), writes=[idt], dma=True)
    E("sync", lambda e: e.dma_start(out=indb[:, :], in_=indI), writes=[indb], dma=True)

    for tt in range(NSLOT):
        xsl = x3.rearrange("(c p) x -> p c x", p=128)[:, :, tt * 512:(tt + 1) * 512]
        E("sync", lambda e, xsl=xsl: e.dma_start(
            out=xot[:, :].rearrange("p (c x) -> p c x", c=DC), in_=xsl),
            writes=[xot], dma=True)
        E("gpsimd", lambda e, xsl=xsl: e.dma_start(
            out=xtb[:, :].rearrange("p (c x) -> p c x", c=DC), in_=xsl),
            writes=[xtb], dma=True)
        for m4 in range(4):
            pg = bk[0]
            for dc in range(DC):
                E("tensor", lambda e, m4=m4, dc=dc, pg=pg: e.matmul(
                    pg[:, 0:8], xot[:, dc * SL + m4 * 128: dc * SL + m4 * 128 + 128],
                    wg[:, dc * 8:(dc + 1) * 8],
                    start=(dc == 0), stop=(dc == DC - 1)),
                    reads=[xot, wg], writes=[pg])
            E("vector", lambda e, pg=pg: e.reduce_max(rmx[:, :], pg[:, 0:8], axis=AX.X),
              reads=[pg], writes=[rmx])
            E("vector", lambda e, pg=pg: e.tensor_scalar(
                mtm[:, :], pg[:, 0:8], rmx[:, 0:1], None, ALU.is_ge),
                reads=[pg, rmx], writes=[mtm])
            E("tensor", lambda e: e.transpose(bk[3][0:8, 0:128], mtm[:, :], idt[:, :]),
              reads=[mtm, idt], writes=[bk[3]])
            E("vector", lambda e, m4=m4: e.tensor_copy(
                maskb[0:8, m4 * 128:(m4 + 1) * 128], bk[3][0:8, 0:128]),
                reads=[bk[3]], writes=[maskb])
        cm.layernorm_nt(xot, bk[6], bk[7], out_f32=None, out_bf16=xhat)
        for ex in range(8):
            for dc in range(DC):
                E("vector", lambda e, ex=ex, dc=dc: e.tensor_scalar(
                    xeb[:, dc * SL:(dc + 1) * SL], xhat[:, dc * SL:(dc + 1) * SL],
                    lns[:, dc * 8 + ex: dc * 8 + ex + 1], lnb[:, dc * 8 + ex: dc * 8 + ex + 1],
                    ALU.mult, ALU.add),
                    reads=[xhat, lns, lnb], writes=[xeb])
            E("tensor", lambda e, ex=ex: e.matmul(
                bk[3][:, :], indb[0:8, ex * 128:(ex + 1) * 128], maskb[0:8, :],
                start=True, stop=True),
                reads=[indb, maskb], writes=[bk[3]])
            for hfc in range(2):
                ph = bk[1 + hfc]
                for dc in range(DC):
                    E("tensor", lambda e, ex=ex, hfc=hfc, dc=dc, ph=ph: e.matmul(
                        ph[:, :], w1e[:, dc * 2048 + ex * 256 + hfc * 128: dc * 2048 + ex * 256 + hfc * 128 + 128],
                        xeb[:, dc * SL:(dc + 1) * SL],
                        start=(dc == 0), stop=(dc == DC - 1)),
                        reads=[w1e, xeb], writes=[ph])
                E("scalar", lambda e, ex=ex, hfc=hfc, ph=ph: e.activation(
                    hall[:, (ex * 2 + hfc) * SL:(ex * 2 + hfc + 1) * SL], ph[:, :], AF.Gelu,
                    bias=b1s[:, hfc * 8 + ex: hfc * 8 + ex + 1]),
                    reads=[ph, b1s], writes=[(hall, (ex, hfc))])
                E("vector", lambda e, ex=ex, hfc=hfc: e.tensor_mul(
                    hall[:, (ex * 2 + hfc) * SL:(ex * 2 + hfc + 1) * SL],
                    hall[:, (ex * 2 + hfc) * SL:(ex * 2 + hfc + 1) * SL], bk[3][:, :]),
                    reads=[(hall, (ex, hfc)), bk[3]], writes=[(hall, (ex, hfc))])
        for fc in range(DC):
            py = bk[4 + fc % 2]
            for ex in range(8):
                for hfc in range(2):
                    E("tensor", lambda e, fc=fc, ex=ex, hfc=hfc, py=py: e.matmul(
                        py[:, :], w2e[:, hfc * 8 * D + ex * D + fc * 128: hfc * 8 * D + ex * D + fc * 128 + 128],
                        hall[:, (ex * 2 + hfc) * SL:(ex * 2 + hfc + 1) * SL],
                        start=(ex == 0 and hfc == 0), stop=False),
                        reads=[w2e, (hall, (ex, hfc))], writes=[py])
            E("tensor", lambda e, fc=fc, py=py: e.matmul(
                py[:, :], b2b[0:8, fc * 128:(fc + 1) * 128], maskb[0:8, :],
                start=False, stop=True),
                reads=[b2b, maskb], writes=[py])
            E("vector", lambda e, fc=fc, py=py: e.tensor_add(
                xres[:, fc * SL:(fc + 1) * SL], py[:, :], xot[:, fc * SL:(fc + 1) * SL]),
                reads=[py, xot], writes=[xres])
        cm.layernorm_nt(xres, bk[6], bk[7], out_f32=xres)
        dstv = xout.rearrange("(c p) x -> p c x", p=128)[:, :, tt * 512:(tt + 1) * 512]
        srcv = xres[:, :].rearrange("p (c x) -> p c x", c=DC)
        E("sync", lambda e, dstv=dstv, srcv=srcv: e.dma_start(out=dstv, in_=srcv),
          reads=[xres], dma=True, tag="out")

    outs = [op for op in em.ops if op.tag == "out"]
    em.emit("sync", lambda e: e.nop(), after=outs, tag="final")
    em.build("D")
    return nc


_NC_CACHE = {}


def _get_nc(name):
    if name not in _NC_CACHE:
        _NC_CACHE[name] = {"A": build_A, "B": build_B, "C1": build_C1,
                           "C2": build_C2, "D": build_D}[name]()
    return _NC_CACHE[name]


def _core_slices(x, c):
    b, par = c // 2, c % 2
    return np.concatenate([x[b, 512 * p:512 * p + 512, :] for p in POS[par]], 0)


def _build_cmask(par):
    m = np.zeros((4, 8, 128, 512), np.float32)
    tri = (np.arange(512)[None, :] >= np.arange(512)[:, None]).astype(np.float32)
    for it in range(4):
        p = POS[par][it]
        for mi, g in enumerate(MSK_TILES[it]):
            q = GPOS[g]
            if q < p:
                m[it, mi * 4:(mi + 1) * 4] = 1.0
            elif q == p:
                for cl in range(4):
                    m[it, mi * 4 + cl] = tri[cl * 128:(cl + 1) * 128, :]
    return m.astype(ml_dtypes.bfloat16)


def kernel(L0_attn_x, L0_attn_W_qkv_v, L0_attn_W_qkv_u, L0_attn_W_o_v,
           L0_attn_W_o_u, L0_ffn_w1, L0_ffn_b1, L0_ffn_w2, L0_ffn_b2,
           L1_attn_wq, L1_attn_wk, L1_attn_wv, L1_ffn_Wg, L1_ffn_ln_scale,
           L1_ffn_ln_bias, L1_ffn_W1, L1_ffn_b1, L1_ffn_W2, L1_ffn_b2):
    f32 = np.float32
    x = np.asarray(L0_attn_x, f32)
    cores = list(range(NCORE))

    WvT = np.ascontiguousarray(np.asarray(L0_attn_W_qkv_v, f32).T)
    WuT = np.ascontiguousarray(np.asarray(L0_attn_W_qkv_u, f32).T)
    WovT = np.ascontiguousarray(np.asarray(L0_attn_W_o_v, f32).T)
    WouT = np.ascontiguousarray(np.asarray(L0_attn_W_o_u, f32).T)
    in_maps = []
    for c in cores:
        own = _core_slices(x, c)
        peer = _core_slices(x, c ^ 1)
        xseq = np.ascontiguousarray(np.concatenate([own, peer], 0).T)
        in_maps.append({"xseq": xseq, "WvT": WvT, "WuT": WuT, "WovT": WovT, "WouT": WouT})
    rA = run_bass_kernel_spmd(_get_nc("A"), in_maps, core_ids=cores)
    x1c = [np.asarray(rA.results[c]["x1"]) for c in cores]

    w1T = np.ascontiguousarray(np.asarray(L0_ffn_w1, f32).T)
    w2T = np.ascontiguousarray(np.asarray(L0_ffn_w2, f32).T)
    b1c = np.ascontiguousarray(np.asarray(L0_ffn_b1, f32).reshape(32, 128).T)
    b2c = np.ascontiguousarray(np.asarray(L0_ffn_b2, f32).reshape(8, 128).T)
    in_maps = [{"x1": x1c[c], "w1T": w1T, "w2T": w2T, "b1c": b1c, "b2c": b2c}
               for c in cores]
    rB = run_bass_kernel_spmd(_get_nc("B"), in_maps, core_ids=cores)
    x2c = [np.asarray(rB.results[c]["x2"]) for c in cores]

    wqT = np.ascontiguousarray(np.asarray(L1_attn_wq, f32).T)
    wkT = np.ascontiguousarray(np.asarray(L1_attn_wk, f32).T)
    wvT = np.ascontiguousarray(np.asarray(L1_attn_wv, f32).T)
    in_maps = [{"x2": x2c[c], "wqT": wqT, "wkT": wkT, "wvT": wvT} for c in cores]
    rC1 = run_bass_kernel_spmd(_get_nc("C1"), in_maps, core_ids=cores)

    in_maps = []
    for c in cores:
        b, par = c // 2, c % 2
        e, o = 2 * b, 2 * b + 1
        Qf = np.concatenate([rC1.results[e]["Qo"], rC1.results[o]["Qo"]], axis=1)
        Vf = np.concatenate([rC1.results[e]["Vo"], rC1.results[o]["Vo"]], axis=0)
        q2f = np.concatenate([rC1.results[e]["q2o"], rC1.results[o]["q2o"]], axis=1)
        in_maps.append({"x2": x2c[c], "Ko": rC1.results[c]["Ko"], "Qf": Qf,
                        "Vf": Vf, "q2f": q2f, "cmask": _build_cmask(par)})
    rC2 = run_bass_kernel_spmd(_get_nc("C2"), in_maps, core_ids=cores)
    x3c = [np.asarray(rC2.results[c]["x3"]) for c in cores]

    Wg = np.asarray(L1_ffn_Wg, f32)
    W1 = np.asarray(L1_ffn_W1, f32)
    W2 = np.asarray(L1_ffn_W2, f32)
    ind = np.zeros((8, 1024), f32)
    for e in range(8):
        ind[e, e * 128:(e + 1) * 128] = 1.0
    shared = {
        "WgT": np.ascontiguousarray(Wg.T),
        "lns": np.ascontiguousarray(np.asarray(L1_ffn_ln_scale, f32).T),
        "lnb": np.ascontiguousarray(np.asarray(L1_ffn_ln_bias, f32).T),
        "W1f": np.ascontiguousarray(W1.transpose(1, 0, 2).reshape(1024, 8 * 256)),
        "b1f": np.ascontiguousarray(np.asarray(L1_ffn_b1, f32).T),
        "W2f": np.ascontiguousarray(W2.transpose(1, 0, 2).reshape(256, 8 * 1024)),
        "b2": np.ascontiguousarray(np.asarray(L1_ffn_b2, f32)),
        "id128": np.eye(128, dtype=f32),
        "ind": ind.astype(ml_dtypes.bfloat16),
    }
    in_maps = [{"x3": x3c[c], **shared} for c in cores]
    rD = run_bass_kernel_spmd(_get_nc("D"), in_maps, core_ids=cores)

    out = np.empty((B, S, D), f32)
    for c in cores:
        b, par = c // 2, c % 2
        got = np.asarray(rD.results[c]["xout"]).T
        for si, p in enumerate(POS[par]):
            out[b, 512 * p:512 * p + 512, :] = got[si * 512:(si + 1) * 512, :]
    return out


# version 04
# speedup vs baseline: 1.0022x; 1.0022x over previous
import numpy as np
import ml_dtypes
import concourse.bass as bass
import concourse.mybir as mybir
from concourse.bass_utils import run_bass_kernel_spmd

F32 = mybir.dt.float32
BF16 = mybir.dt.bfloat16
AF = mybir.ActivationFunctionType
ALU = mybir.AluOpType
AX = mybir.AxisListType

D = 1024
DC = 8
S = 4096
B = 4
TOK = 2048
NSLOT = 4
SL = 512
NCORE = 8
POS = [[0, 3, 4, 7], [1, 2, 5, 6]]
LN_EPS = 1e-5

ENGINES = ("tensor", "vector", "scalar", "gpsimd", "sync")
DEBUG_A = False


class _Op:
    __slots__ = ("eng", "seq", "fn", "reads", "writes", "is_dma", "needs_inc",
                 "waits", "inc_count", "tag", "dma_slot", "dma_use")

    def __init__(self, eng, seq, fn, reads, writes, is_dma, tag):
        self.eng = eng
        self.seq = seq
        self.fn = fn
        self.reads = reads
        self.writes = writes
        self.is_dma = is_dma
        self.needs_inc = False
        self.waits = {}
        self.inc_count = 0
        self.tag = tag
        self.dma_slot = -1
        self.dma_use = 0


def _rkey(r):
    sub = None
    if isinstance(r, tuple):
        r, sub = r
    h = getattr(r, "tensor", r)
    name = getattr(h, "name", None)
    if name is None:
        name = id(h)
    return (name, sub)


NDMASEM = 8


class Emitter:

    def __init__(self, nc):
        self.nc = nc
        self.ops = []
        self.last_writer = {}
        self.readers = {}
        self.per_eng = {e: [] for e in ENGINES}
        self.alias = {}
        self.dma_count = {e: 0 for e in ENGINES}

    def set_alias(self, tensors, group):
        for t in tensors:
            self.alias[getattr(t, "name", t)] = group

    def _key(self, r):
        name, sub = _rkey(r)
        g = self.alias.get(name)
        if g is not None:
            return (g, None)
        return (name, sub)

    def emit(self, eng, fn, reads=(), writes=(), dma=False, after=(), tag=""):
        op = _Op(eng, len(self.ops), fn, [self._key(r) for r in reads],
                 [self._key(r) for r in writes], dma, tag)
        if dma:
            i = self.dma_count[eng]
            self.dma_count[eng] = i + 1
            op.dma_slot = i % NDMASEM
            op.dma_use = i // NDMASEM + 1
        deps = list(after)
        for r in op.reads:
            w = self.last_writer.get(r)
            if w is not None:
                deps.append(w)
        for r in op.writes:
            w = self.last_writer.get(r)
            if w is not None:
                deps.append(w)
            deps.extend(self.readers.get(r, ()))
        for d in deps:
            if d is op:
                continue
            if d.eng == eng == "tensor" and not d.is_dma and not dma:
                continue
            ch = ("d", d.eng, d.dma_slot) if d.is_dma else ("e", d.eng)
            cur = op.waits.get(ch)
            if cur is None or d.seq > cur.seq:
                op.waits[ch] = d
        for r in op.writes:
            self.last_writer[r] = op
            self.readers[r] = []
        for r in op.reads:
            self.readers.setdefault(r, []).append(op)
        self.ops.append(op)
        self.per_eng[eng].append(op)
        return op

    def build(self, block_name=None):
        nc = self.nc
        for op in self.ops:
            for d in op.waits.values():
                if not d.is_dma:
                    d.needs_inc = True
        sem_final = {}
        for e in ENGINES:
            c = 0
            for op in self.per_eng[e]:
                if op.needs_inc and not op.is_dma:
                    c += 1
                op.inc_count = c
            sem_final[e] = c
        sems = {e: nc.alloc_semaphore(f"em_{e}") for e in ENGINES}
        dsems = {e: [nc.alloc_semaphore(f"dm_{e}{i}") for i in range(NDMASEM)]
                 for e in ENGINES if self.dma_count[e] > 0}

        def _resolve(ch, dop):
            if ch[0] == "d":
                return dsems[ch[1]][ch[2]], 16 * dop.dma_use
            return sems[ch[1]], dop.inc_count

        def runner(e):
            def run(engine):
                clock = {}
                for op in self.per_eng[e]:
                    if op.is_dma and op.dma_use > 1:
                        ch = ("d", e, op.dma_slot)
                        v = 16 * (op.dma_use - 1)
                        if clock.get(ch, 0) < v:
                            engine.wait_ge(dsems[e][op.dma_slot], v)
                            clock[ch] = v
                    for ch in sorted(op.waits, key=str):
                        sem, v = _resolve(ch, op.waits[ch])
                        if v > clock.get(ch, 0):
                            engine.wait_ge(sem, v)
                            clock[ch] = v
                    inst = op.fn(engine)
                    if op.is_dma:
                        inst.then_inc(dsems[e][op.dma_slot], 16)
                    elif op.needs_inc:
                        inst.then_inc(sems[e], 1)
            return run

        with nc.Block(block_name) as blk:
            blk.tensor(runner("tensor"))
            blk.vector(runner("vector"))
            blk.scalar(runner("scalar"))
            blk.gpsimd(runner("gpsimd"))
            blk.sync(runner("sync"))
        return sems, sem_final


def _mk_nc():
    return bass.Bass("TRN2", target_bir_lowering=False)


def _in(nc, name, shape, dtype=F32):
    return nc.dram_tensor(name, shape, dtype, kind="ExternalInput").ap()


def _out(nc, name, shape, dtype=F32):
    return nc.dram_tensor(name, shape, dtype, kind="ExternalOutput").ap()


def load_fm(em, dst, src, chunks, width, eng="sync", tag="ld"):
    ap = src.rearrange("(c p) x -> p c x", p=128)
    dv = dst[:, :].rearrange("p (c x) -> p c x", c=chunks)
    em.emit(eng, lambda e: e.dma_start(out=dv, in_=ap), writes=[dst], dma=True, tag=tag)


def store_fm(em, dst, src_sb, chunks, width, col0=0, eng="sync", tag="st", reads=()):
    ap = dst.rearrange("(c p) x -> p c x", p=128)
    sv = src_sb[:, col0:col0 + chunks * width].rearrange("p (c x) -> p c x", c=chunks)
    return em.emit(eng, lambda e: e.dma_start(out=ap, in_=sv),
                   reads=list(reads) or [src_sb], dma=True, tag=tag)


class Common:

    def __init__(self, nc, em):
        self.nc, self.em = nc, em
        self.ones_mean = nc.alloc_sbuf_tensor("ones_mean", [128, 1], BF16)
        self.ones_row = nc.alloc_sbuf_tensor("ones_row", [1, 128], F32)
        self.ones_rowb = nc.alloc_sbuf_tensor("ones_rowb", [1, 128], BF16)
        self.ones_colb = nc.alloc_sbuf_tensor("ones_colb", [128, 1], BF16)
        em.emit("vector", lambda e: e.memset(self.ones_mean[:, :], 1.0 / D),
                writes=[self.ones_mean])
        em.emit("vector", lambda e: e.memset(self.ones_row[:, :], 1.0),
                writes=[self.ones_row])
        em.emit("vector", lambda e: e.memset(self.ones_rowb[:, :], 1.0),
                writes=[self.ones_rowb])
        em.emit("vector", lambda e: e.memset(self.ones_colb[:, :], 1.0),
                writes=[self.ones_colb])
        self.eps = nc.alloc_sbuf_tensor("eps_c", [1, 1], F32)
        em.emit("vector", lambda e: e.memset(self.eps[:, :], float(LN_EPS)),
                writes=[self.eps])
        self.xrb = self.x2b = self.rows = None
        self.Ab = nc.alloc_sbuf_tensor("ln_Ab", [128, SL], F32)
        self.Bb = nc.alloc_sbuf_tensor("ln_Bb", [128, SL], F32)
        self.tmp = nc.alloc_sbuf_tensor("ln_tmp", [128, SL], F32)

    def attach_ln_scratch(self, xrb, x2b, rows):
        self.xrb, self.x2b, self.rows = xrb, x2b, rows

    def layernorm_nt(self, xres, bk_stat, bk_bcast, out_f32=None, out_bf16=None):
        em = self.em
        R = self.rows
        mrow, erow = R[0:1, 0:SL], R[0:1, SL:2 * SL]
        m2 = var = lnv = R[0:1, 2 * SL:3 * SL]
        rstd = R[0:1, 3 * SL:4 * SL]
        mr = negmr = R[0:1, 2 * SL:3 * SL]
        for fc in range(DC):
            sl = slice(fc * SL, (fc + 1) * SL)
            em.emit("vector", lambda e, sl=sl: e.tensor_copy(self.xrb[:, sl], xres[:, sl]),
                    reads=[xres], writes=[self.xrb])
            em.emit("scalar", lambda e, sl=sl: e.activation(self.x2b[:, sl], xres[:, sl], AF.Square),
                    reads=[xres], writes=[self.x2b])
        for fc in range(DC):
            sl = slice(fc * SL, (fc + 1) * SL)
            em.emit("tensor", lambda e, sl=sl, fc=fc: e.matmul(
                bk_stat[0:1, :], self.ones_mean[:, :], self.xrb[:, sl],
                start=(fc == 0), stop=(fc == DC - 1)),
                reads=[self.xrb, self.ones_mean], writes=[bk_stat])
            em.emit("tensor", lambda e, sl=sl, fc=fc: e.matmul(
                bk_bcast[0:1, :], self.ones_mean[:, :], self.x2b[:, sl],
                start=(fc == 0), stop=(fc == DC - 1)),
                reads=[self.x2b, self.ones_mean], writes=[bk_bcast])
        em.emit("vector", lambda e: e.tensor_copy(mrow, bk_stat[0:1, :]),
                reads=[bk_stat], writes=[R])
        em.emit("vector", lambda e: e.tensor_copy(erow, bk_bcast[0:1, :]),
                reads=[bk_bcast], writes=[R])
        em.emit("vector", lambda e: e.tensor_mul(m2, mrow, mrow), reads=[R], writes=[R])
        em.emit("vector", lambda e: e.tensor_sub(var, erow, m2), reads=[R], writes=[R])
        em.emit("scalar", lambda e: e.activation(lnv, var, AF.Ln, bias=self.eps[0:1, 0:1]),
                reads=[R, self.eps], writes=[R])
        em.emit("scalar", lambda e: e.activation(rstd, lnv, AF.Exp, scale=-0.5),
                reads=[R], writes=[R])
        em.emit("vector", lambda e: e.tensor_mul(mr, mrow, rstd), reads=[R], writes=[R])
        em.emit("vector", lambda e: e.tensor_scalar(negmr, mr, -1.0, None, ALU.mult),
                reads=[R], writes=[R])
        em.emit("tensor", lambda e: e.matmul(bk_stat[:, :], self.ones_row[:, :], rstd,
                                             start=True, stop=True),
                reads=[R, self.ones_row], writes=[bk_stat])
        em.emit("vector", lambda e: e.tensor_copy(self.Ab[:, :], bk_stat[:, :]),
                reads=[bk_stat], writes=[self.Ab])
        em.emit("tensor", lambda e: e.matmul(bk_bcast[:, :], self.ones_row[:, :], negmr,
                                             start=True, stop=True),
                reads=[R, self.ones_row], writes=[bk_bcast])
        em.emit("vector", lambda e: e.tensor_copy(self.Bb[:, :], bk_bcast[:, :]),
                reads=[bk_bcast], writes=[self.Bb])
        for fc in range(DC):
            sl = slice(fc * SL, (fc + 1) * SL)
            em.emit("vector", lambda e, sl=sl: e.tensor_mul(self.tmp[:, :], xres[:, sl], self.Ab[:, :]),
                    reads=[xres, self.Ab], writes=[self.tmp])
            if out_f32 is not None:
                em.emit("vector", lambda e, sl=sl: e.tensor_add(out_f32[:, sl], self.tmp[:, :], self.Bb[:, :]),
                        reads=[self.tmp, self.Bb], writes=[out_f32])
                if out_bf16 is not None:
                    em.emit("scalar", lambda e, sl=sl: e.activation(out_bf16[:, sl], out_f32[:, sl], AF.Copy),
                            reads=[out_f32], writes=[out_bf16])
            elif out_bf16 is not None:
                em.emit("vector", lambda e, sl=sl: e.tensor_add(out_bf16[:, sl], self.tmp[:, :], self.Bb[:, :]),
                        reads=[self.tmp, self.Bb], writes=[out_bf16])


def build_A():
    nc = _mk_nc()
    xseq = _in(nc, "xseq", [D, S])
    WvT = _in(nc, "WvT", [D, 256])
    WuT = _in(nc, "WuT", [256, 3 * D])
    WovT = _in(nc, "WovT", [D, 256])
    WouT = _in(nc, "WouT", [256, D])
    x1 = _out(nc, "x1", [D, TOK])

    em = Emitter(nc)
    cm = Common(nc, em)

    xb = nc.alloc_sbuf_tensor("xb", [128, DC * S], BF16)
    xb_off = nc.lookup_mloc(xb).addr
    xres = nc.alloc_sbuf_tensor_at("xres", [128, DC * SL], F32, offset=xb_off)
    xot = nc.alloc_sbuf_tensor_at("xot", [128, DC * SL], F32, offset=xb_off + 16384)
    xrb = nc.alloc_sbuf_tensor_at("ln_xrb", [128, DC * SL], BF16, offset=xb_off + 32768)
    x2b = nc.alloc_sbuf_tensor_at("ln_x2b", [128, DC * SL], BF16, offset=xb_off + 40960)
    lrows = nc.alloc_sbuf_tensor_at("ln_rows", [1, 4 * SL], F32, offset=xb_off + 49152)
    em.set_alias([xb, xres, xot, xrb, x2b, lrows], "R1")
    cm.attach_ln_scratch(xrb, x2b, lrows)

    wv = nc.alloc_sbuf_tensor("wv", [128, DC * 256], BF16)
    wu = nc.alloc_sbuf_tensor("wu", [128, 2 * 3 * D], BF16)
    wov = nc.alloc_sbuf_tensor("wov", [128, DC * 256], BF16)
    wou = nc.alloc_sbuf_tensor("wou", [128, 2 * D], BF16)
    t = nc.alloc_sbuf_tensor("t", [128, 2 * S], BF16)
    Qg = nc.alloc_sbuf_tensor("Qg", [128, S], BF16)
    Kg = nc.alloc_sbuf_tensor("Kg", [128, TOK], BF16)
    Vg = nc.alloc_sbuf_tensor("Vg", [128, 32 * 130], BF16)
    y = nc.alloc_sbuf_tensor("y", [128, DC * TOK], BF16)
    z1 = nc.alloc_sbuf_tensor("z1", [128, 2 * TOK], BF16)
    Pp = nc.alloc_sbuf_tensor("Pp", [128, 4 * SL], BF16)
    sq = nc.alloc_sbuf_tensor("sq", [128, S], BF16)
    r_sc = nc.alloc_sbuf_tensor("r_sc", [1, 528], F32)
    negc = nc.alloc_sbuf_tensor("negc", [128, 1], F32)
    rl = nc.alloc_sbuf_tensor("rl", [1, SL], F32)
    rlb = nc.alloc_sbuf_tensor("rlb", [64, SL], F32)

    bk = [nc.alloc_psum_tensor(f"bk{i}", [128, 512], F32) for i in range(8)]

    E = em.emit

    load_fm(em, xb, xseq, DC, S, eng="gpsimd")
    load_fm(em, wv, WvT, DC, 256, eng="gpsimd")
    load_fm(em, wu, WuT, 2, 3 * D, eng="gpsimd")
    load_fm(em, wov, WovT, DC, 256, eng="gpsimd")
    load_fm(em, wou, WouT, 2, D, eng="gpsimd")

    for rc in range(2):
        for ntile in range(S // 512):
            pb = bk[ntile % 2]
            for dc in range(DC):
                E("tensor", lambda e, rc=rc, nt=ntile, dc=dc, pb=pb: e.matmul(
                    pb[:, :], wv[:, dc * 256 + rc * 128: dc * 256 + rc * 128 + 128],
                    xb[:, dc * S + nt * 512: dc * S + nt * 512 + 512],
                    start=(dc == 0), stop=(dc == DC - 1)),
                    reads=[wv, xb], writes=[pb])
            E("scalar", lambda e, rc=rc, nt=ntile, pb=pb: e.activation(
                t[:, rc * S + nt * 512: rc * S + nt * 512 + 512], pb[:, :], AF.Copy),
                reads=[pb], writes=[(t, (rc, ntile))])

    for g in range(8):
        fq = D + g * 128
        fk = g * 128
        fv = 2 * D + g * 128
        for ntile in range(S // 512):
            pb = bk[ntile % 2]
            for rc in range(2):
                E("tensor", lambda e, nt=ntile, rc=rc, pb=pb, fq=fq: e.matmul(
                    pb[:, :], wu[:, rc * 3 * D + fq: rc * 3 * D + fq + 128],
                    t[:, rc * S + nt * 512: rc * S + nt * 512 + 512],
                    start=(rc == 0), stop=(rc == 1)),
                    reads=[wu, (t, (rc, ntile))], writes=[pb])
            E("scalar", lambda e, nt=ntile, pb=pb: e.activation(
                Qg[:, nt * 512: nt * 512 + 512], pb[:, :], AF.Copy),
                reads=[pb], writes=[(Qg, ntile)])
        for ntile in range(TOK // 512):
            pb = bk[2 + ntile % 2]
            for rc in range(2):
                E("tensor", lambda e, nt=ntile, rc=rc, pb=pb, fk=fk: e.matmul(
                    pb[:, :], wu[:, rc * 3 * D + fk: rc * 3 * D + fk + 128],
                    t[:, rc * S + nt * 512: rc * S + nt * 512 + 512],
                    start=(rc == 0), stop=(rc == 1)),
                    reads=[wu, (t, (rc, ntile))], writes=[pb])
            E("scalar", lambda e, nt=ntile, pb=pb: e.activation(
                Kg[:, nt * 512: nt * 512 + 512], pb[:, :], AF.Copy),
                reads=[pb], writes=[(Kg, ntile)])
        for jc in range(32):
            pb = bk[4 + jc % 2]
            for rc in range(2):
                E("tensor", lambda e, jc=jc, rc=rc, pb=pb, fv=fv: e.matmul(
                    pb[:, 0:128], t[:, rc * S + jc * 128: rc * S + jc * 128 + 128],
                    wu[:, rc * 3 * D + fv: rc * 3 * D + fv + 128],
                    start=(rc == 0), stop=(rc == 1)),
                    reads=[wu, (t, (rc, jc // 4))], writes=[pb])
            for h in range(2):
                E("vector", lambda e, jc=jc, h=h, pb=pb: e.tensor_copy(
                    Vg[:, jc * 130 + h * 65: jc * 130 + h * 65 + 64],
                    pb[:, h * 64: h * 64 + 64]),
                    reads=[pb], writes=[(Vg, jc)])
        E("vector", lambda e: e.memset(
            Vg[:, :].rearrange("p (j h c) -> p j h c", j=32, h=2)[:, :, :, 64:65], 1.0),
            writes=[(Vg, "ones")])

        scr = r_sc[0:1, 0:512]
        mk2 = r_sc[0:1, 512:513]
        mq2 = r_sc[0:1, 513:514]
        prd = r_sc[0:1, 514:515]
        sqr = r_sc[0:1, 515:516]
        ngc = r_sc[0:1, 516:517]
        cur = r_sc[0:1, 517:518]
        E("vector", lambda e: e.memset(mk2, 0.0), writes=[(r_sc, "mk2")])
        E("vector", lambda e: e.memset(mq2, 0.0), writes=[(r_sc, "mq2")])
        for src_, nmax, mrow, mkey in ((Kg, TOK // 512, mk2, "mk2"), (Qg, S // 512, mq2, "mq2")):
            for ntile in range(nmax):
                E("scalar", lambda e, nt=ntile, src_=src_: e.activation(
                    sq[:, nt * 512: nt * 512 + 512], src_[:, nt * 512: nt * 512 + 512], AF.Square),
                    reads=[(src_, ntile)], writes=[(sq, ntile)])
            for ntile in range(nmax):
                pb = bk[6]
                E("tensor", lambda e, nt=ntile, pb=pb: e.matmul(
                    pb[0:1, :], cm.ones_colb[:, :], sq[:, nt * 512: nt * 512 + 512],
                    start=True, stop=True),
                    reads=[(sq, ntile), cm.ones_colb], writes=[pb])
                E("vector", lambda e, pb=pb: e.tensor_copy(scr, pb[0:1, :]),
                    reads=[pb], writes=[(r_sc, "scr")])
                E("vector", lambda e: e.reduce_max(cur, scr, axis=AX.X),
                    reads=[(r_sc, "scr")], writes=[(r_sc, "cur")])
                E("vector", lambda e, mrow=mrow, mkey=mkey: e.tensor_max(mrow, mrow, cur),
                    reads=[(r_sc, "cur"), (r_sc, mkey)], writes=[(r_sc, mkey)])
        E("vector", lambda e: e.tensor_mul(prd, mk2, mq2),
          reads=[(r_sc, "mk2"), (r_sc, "mq2")], writes=[(r_sc, "prd")])
        E("scalar", lambda e: e.activation(sqr, prd, AF.Sqrt),
          reads=[(r_sc, "prd")], writes=[(r_sc, "sqr")])
        E("vector", lambda e: e.tensor_scalar(ngc, sqr, -0.125, None, ALU.mult),
          reads=[(r_sc, "sqr")], writes=[(r_sc, "ngc")])
        pb = bk[6]
        E("tensor", lambda e, pb=pb: e.matmul(pb[:, 0:1], cm.ones_row[:, :], ngc,
                                              start=True, stop=True),
          reads=[(r_sc, "ngc"), cm.ones_row], writes=[pb])
        E("vector", lambda e, pb=pb: e.tensor_copy(negc[:, :], pb[:, 0:1]),
          reads=[pb], writes=[negc])

        for it in range(NSLOT):
            nb = [bk[4], bk[5]]
            for h in range(2):
                pbase = h * 64
                nbank = nb[h]
                for jc in range(32):
                    sb_ = bk[2 * h + (jc % 2)]
                    E("tensor", lambda e, jc=jc, it=it, pbase=pbase, sb_=sb_: e.matmul(
                        sb_[:, :], Qg[pbase:pbase + 64, jc * 128: jc * 128 + 128],
                        Kg[pbase:pbase + 64, it * 512: it * 512 + 512],
                        start=True, stop=True),
                        reads=[(Qg, jc // 4), (Kg, it)], writes=[sb_])
                    psl = slice((2 * h + (jc % 2)) * SL, (2 * h + (jc % 2)) * SL + SL)
                    E("scalar", lambda e, psl=psl, sb_=sb_: e.activation(
                        Pp[:, psl], sb_[:, :], AF.Exp, bias=negc[:, 0:1], scale=0.125),
                        reads=[sb_, negc], writes=[(Pp, psl.start)])
                    E("tensor", lambda e, jc=jc, h=h, psl=psl, nbank=nbank: e.matmul(
                        nbank[0:65, :], Vg[:, jc * 130 + h * 65: jc * 130 + h * 65 + 65],
                        Pp[:, psl], start=(jc == 0), stop=(jc == 31)),
                        reads=[(Vg, jc), (Vg, "ones"), (Pp, psl.start)], writes=[nbank])
            for h in range(2):
                nbank = nb[h]
                E("vector", lambda e, nbank=nbank: e.reciprocal(rl[0:1, :], nbank[64:65, :]),
                  reads=[nbank], writes=[rl])
                pb2 = bk[6]
                E("tensor", lambda e, pb2=pb2: e.matmul(pb2[0:64, :], cm.ones_row[0:1, 0:64], rl[0:1, :],
                                                        start=True, stop=True),
                  reads=[rl, cm.ones_row], writes=[pb2])
                E("vector", lambda e, pb2=pb2: e.tensor_copy(rlb[:, :], pb2[0:64, :]),
                  reads=[pb2], writes=[rlb])
                hh = 2 * g + h
                fc_, pb_ = hh // 2, (hh % 2) * 64
                E("vector", lambda e, it=it, fc_=fc_, pb_=pb_, nbank=nbank: e.tensor_mul(
                    y[pb_:pb_ + 64, fc_ * TOK + it * 512: fc_ * TOK + it * 512 + 512],
                    nbank[0:64, :], rlb[:, :]),
                    reads=[nbank, rlb], writes=[(y, (fc_, pb_, it))])

    for rc in range(2):
        for ntile in range(NSLOT):
            pb = bk[ntile % 2]
            for dc in range(DC):
                E("tensor", lambda e, rc=rc, nt=ntile, dc=dc, pb=pb: e.matmul(
                    pb[:, :], wov[:, dc * 256 + rc * 128: dc * 256 + rc * 128 + 128],
                    y[:, dc * TOK + nt * 512: dc * TOK + nt * 512 + 512],
                    start=(dc == 0), stop=(dc == DC - 1)),
                    reads=[wov, (y, (dc, 0, ntile)), (y, (dc, 64, ntile))], writes=[pb])
            E("scalar", lambda e, rc=rc, nt=ntile, pb=pb: e.activation(
                z1[:, rc * TOK + nt * 512: rc * TOK + nt * 512 + 512], pb[:, :], AF.Copy),
                reads=[pb], writes=[(z1, (rc, ntile))])

    for ntile in range(NSLOT):
        xsl = xseq.rearrange("(c p) x -> p c x", p=128)[:, :, ntile * 512:(ntile + 1) * 512]
        E("sync", lambda e, xsl=xsl: e.dma_start(
            out=xot[:, :].rearrange("p (c x) -> p c x", c=DC), in_=xsl),
            writes=[xot], dma=True)
        for fc in range(DC):
            pb = bk[2 + fc % 2]
            for rc in range(2):
                E("tensor", lambda e, fc=fc, nt=ntile, rc=rc, pb=pb: e.matmul(
                    pb[:, :], wou[:, rc * D + fc * 128: rc * D + fc * 128 + 128],
                    z1[:, rc * TOK + nt * 512: rc * TOK + nt * 512 + 512],
                    start=(rc == 0), stop=(rc == 1)),
                    reads=[wou, (z1, (rc, ntile))], writes=[pb])
            E("vector", lambda e, fc=fc, pb=pb: e.tensor_add(
                xres[:, fc * SL:(fc + 1) * SL], pb[:, :], xot[:, fc * SL:(fc + 1) * SL]),
                reads=[pb, xot], writes=[xres])
        cm.layernorm_nt(xres, bk[6], bk[7], out_f32=xres)
        dstv = x1.rearrange("(c p) x -> p c x", p=128)[:, :, ntile * 512:(ntile + 1) * 512]
        srcv = xres[:, :].rearrange("p (c x) -> p c x", c=DC)
        E("sync", lambda e, dstv=dstv, srcv=srcv: e.dma_start(out=dstv, in_=srcv),
          reads=[xres], dma=True, tag="out")

    if DEBUG_A:
        dt_ = _out(nc, "dbg_t", [2 * 128, S], BF16)
        dy = _out(nc, "dbg_y", [D, TOK], BF16)
        dz1 = _out(nc, "dbg_z1", [2 * 128, TOK], BF16)
        store_fm(em, dt_, t, 2, S, eng="sync", tag="out", reads=[(t, (rc_, nt_)) for rc_ in range(2) for nt_ in range(8)])
        store_fm(em, dy, y, DC, TOK, eng="sync", tag="out",
                 reads=[(y, (fc_, pb_, it_)) for fc_ in range(DC) for pb_ in (0, 64) for it_ in range(4)])
        store_fm(em, dz1, z1, 2, TOK, eng="sync", tag="out",
                 reads=[(z1, (rc_, nt_)) for rc_ in range(2) for nt_ in range(4)])
    outs = [op for op in em.ops if op.tag == "out"]
    em.emit("sync", lambda e: e.nop(), after=outs, tag="final")
    em.build("A")
    return nc


def build_B():
    nc = _mk_nc()
    x1 = _in(nc, "x1", [D, TOK])
    w1T = _in(nc, "w1T", [D, 4096])
    w2T = _in(nc, "w2T", [4096, D])
    b1c = _in(nc, "b1c", [128, 32])
    b2c = _in(nc, "b2c", [128, 8])
    x2 = _out(nc, "x2", [D, TOK])
    s1s = nc.dram_tensor("s1s", [32, 128, 1024], BF16, kind="Internal").ap()
    s2s = nc.dram_tensor("s2s", [4096, D], BF16, kind="Internal").ap()

    em = Emitter(nc)
    cm = Common(nc, em)
    xrb = nc.alloc_sbuf_tensor("ln_xrb", [128, DC * SL], BF16)
    x2b = nc.alloc_sbuf_tensor("ln_x2b", [128, DC * SL], BF16)
    lrows = nc.alloc_sbuf_tensor("ln_rows", [1, 4 * SL], F32)
    cm.attach_ln_scratch(xrb, x2b, lrows)

    xsb = nc.alloc_sbuf_tensor("xsb", [128, DC * TOK], BF16)
    h = nc.alloc_sbuf_tensor("h", [128, 32 * SL], BF16)
    xot = nc.alloc_sbuf_tensor("xot", [128, DC * SL], F32)
    xres = nc.alloc_sbuf_tensor("xres", [128, DC * SL], F32)
    wblk = nc.alloc_sbuf_tensor("wblk", [128, 1024], F32)
    sgn = nc.alloc_sbuf_tensor("sgn", [128, 1024], BF16)
    ab = nc.alloc_sbuf_tensor("ab", [128, 1024], BF16)
    s1buf = [nc.alloc_sbuf_tensor(f"s1b{i}", [128, 1024], BF16) for i in range(2)]
    s2buf = [nc.alloc_sbuf_tensor(f"s2b{i}", [128, 1024], BF16) for i in range(2)]
    a1c = nc.alloc_sbuf_tensor("a1c", [128, 32], F32)
    a2c = nc.alloc_sbuf_tensor("a2c", [128, 8], F32)
    b1s = nc.alloc_sbuf_tensor("b1s", [128, 32], F32)
    b2s = nc.alloc_sbuf_tensor("b2s", [128, 8], F32)
    ot = nc.alloc_sbuf_tensor("ot", [128, SL], F32)
    ones4k = nc.alloc_sbuf_tensor("ones4k", [128, 1], BF16)
    bk = [nc.alloc_psum_tensor(f"bk{i}", [128, 512], F32) for i in range(8)]
    E = em.emit

    E("vector", lambda e: e.memset(ones4k[:, :], 1.0 / 4096.0), writes=[ones4k])
    E("sync", lambda e: e.dma_start(out=b1s[:, :], in_=b1c), writes=[b1s], dma=True)
    E("sync", lambda e: e.dma_start(out=b2s[:, :], in_=b2c), writes=[b2s], dma=True)
    load_fm(em, xsb, x1, DC, TOK, eng="gpsimd")

    for hfc in range(32):
        wv_ = w1T.rearrange("(dc p) c -> p dc c", p=128)[:, :, hfc * 128:(hfc + 1) * 128]
        E("sync", lambda e, wv_=wv_: e.dma_start(
            out=wblk[:, :].rearrange("p (dc c) -> p dc c", dc=8), in_=wv_),
            writes=[wblk], dma=True)
        E("scalar", lambda e: e.activation(sgn[:, :], wblk[:, :], AF.Sign),
          reads=[wblk], writes=[sgn])
        E("scalar", lambda e: e.activation(ab[:, :], wblk[:, :], AF.Abs),
          reads=[wblk], writes=[ab])
        E("gpsimd", lambda e, hfc=hfc: e.dma_start(out=s1s[hfc], in_=sgn[:, :]),
          reads=[sgn], writes=[(s1s, hfc)], dma=True)
        pb = bk[0]
        for dc in range(DC):
            E("tensor", lambda e, dc=dc, pb=pb: e.matmul(
                pb[:, 0:1], ab[:, dc * 128:(dc + 1) * 128], cm.ones_mean[:, :],
                start=(dc == 0), stop=(dc == DC - 1)),
                reads=[ab, cm.ones_mean], writes=[pb])
        E("vector", lambda e, hfc=hfc, pb=pb: e.tensor_copy(a1c[:, hfc:hfc + 1], pb[:, 0:1]),
          reads=[pb], writes=[a1c])

    pa = bk[1]
    E("vector", lambda e: e.memset(a2c[:, :], 0.0), writes=[a2c])
    for hfc in range(32):
        wv_ = w2T[hfc * 128:(hfc + 1) * 128, :]
        E("sync", lambda e, wv_=wv_: e.dma_start(out=wblk[:, :], in_=wv_),
          writes=[wblk], dma=True)
        E("scalar", lambda e: e.activation(sgn[:, :], wblk[:, :], AF.Sign),
          reads=[wblk], writes=[sgn])
        E("scalar", lambda e: e.activation(ab[:, :], wblk[:, :], AF.Abs),
          reads=[wblk], writes=[ab])
        E("gpsimd", lambda e, hfc=hfc: e.dma_start(out=s2s[hfc * 128:(hfc + 1) * 128, :], in_=sgn[:, :]),
          reads=[sgn], writes=[(s2s, hfc)], dma=True)
        for fc in range(DC):
            E("tensor", lambda e, fc=fc, pa=pa: e.matmul(
                pa[:, fc:fc + 1], ab[:, fc * 128:(fc + 1) * 128], ones4k[:, :],
                start=True, stop=True),
                reads=[ab, ones4k], writes=[pa])
        E("vector", lambda e, pa=pa: e.tensor_add(a2c[:, :], a2c[:, :], pa[:, 0:8]),
          reads=[pa, a2c], writes=[a2c])

    for st in range(4):
        xsl = x1.rearrange("(c p) x -> p c x", p=128)[:, :, st * 512:(st + 1) * 512]
        E("sync", lambda e, xsl=xsl: e.dma_start(
            out=xot[:, :].rearrange("p (c x) -> p c x", c=DC), in_=xsl),
            writes=[xot], dma=True)
        for hfc in range(32):
            sb_ = s1buf[hfc % 2]
            E("sync", lambda e, hfc=hfc, sb_=sb_: e.dma_start(out=sb_[:, :], in_=s1s[hfc]),
              reads=[(s1s, hfc)], writes=[sb_], dma=True)
            pb = bk[hfc % 2]
            for dc in range(DC):
                E("tensor", lambda e, dc=dc, st=st, sb_=sb_, pb=pb: e.matmul(
                    pb[:, :], sb_[:, dc * 128:(dc + 1) * 128],
                    xsb[:, dc * TOK + st * 512: dc * TOK + st * 512 + 512],
                    start=(dc == 0), stop=(dc == DC - 1)),
                    reads=[sb_, xsb], writes=[pb])
            E("scalar", lambda e, hfc=hfc, pb=pb: e.activation(
                h[:, hfc * SL:(hfc + 1) * SL], pb[:, :], AF.Gelu,
                bias=b1s[:, hfc:hfc + 1], scale=a1c[:, hfc:hfc + 1]),
                reads=[pb, b1s, a1c], writes=[(h, hfc)])
        for hfc in range(32):
            sb_ = s2buf[hfc % 2]
            E("sync", lambda e, hfc=hfc, sb_=sb_: e.dma_start(
                out=sb_[:, :], in_=s2s[hfc * 128:(hfc + 1) * 128, :]),
                reads=[(s2s, hfc)], writes=[sb_], dma=True)
            for fc in range(DC):
                E("tensor", lambda e, fc=fc, hfc=hfc, sb_=sb_: e.matmul(
                    bk[fc][:, :], sb_[:, fc * 128:(fc + 1) * 128],
                    h[:, hfc * SL:(hfc + 1) * SL],
                    start=(hfc == 0), stop=(hfc == 31)),
                    reads=[sb_, (h, hfc)], writes=[bk[fc]])
        for fc in range(DC):
            E("vector", lambda e, fc=fc: e.tensor_scalar(
                ot[:, :], bk[fc][:, :], a2c[:, fc:fc + 1], b2s[:, fc:fc + 1],
                ALU.mult, ALU.add),
                reads=[bk[fc], b2s, a2c], writes=[ot])
            E("vector", lambda e, fc=fc: e.tensor_add(
                xres[:, fc * SL:(fc + 1) * SL], ot[:, :], xot[:, fc * SL:(fc + 1) * SL]),
                reads=[ot, xot], writes=[xres])
        cm.layernorm_nt(xres, bk[0], bk[1], out_f32=xres)
        dstv = x2.rearrange("(c p) x -> p c x", p=128)[:, :, st * 512:(st + 1) * 512]
        srcv = xres[:, :].rearrange("p (c x) -> p c x", c=DC)
        E("sync", lambda e, dstv=dstv, srcv=srcv: e.dma_start(out=dstv, in_=srcv),
          reads=[xres], dma=True, tag="out")

    outs = [op for op in em.ops if op.tag == "out"]
    em.emit("sync", lambda e: e.nop(), after=outs, tag="final")
    em.build("B")
    return nc


GPOS = POS[0] + POS[1]
INC_TILES = [list(range(0, it + 1)) + list(range(4, 4 + it + 1)) for it in range(4)]
MSK_TILES = [[it, 4 + it] for it in range(4)]


def build_C1():
    nc = _mk_nc()
    x2 = _in(nc, "x2", [D, TOK])
    wqT = _in(nc, "wqT", [D, D])
    wkT = _in(nc, "wkT", [D, D])
    wvT = _in(nc, "wvT", [D, D])
    Qo = _out(nc, "Qo", [D, TOK], BF16)
    Ko = _out(nc, "Ko", [D, TOK], BF16)
    Vo = _out(nc, "Vo", [TOK, D], BF16)
    q2o = _out(nc, "q2o", [1, TOK])

    em = Emitter(nc)
    cm = Common(nc, em)
    xsb = nc.alloc_sbuf_tensor("xsb", [128, DC * TOK], BF16)
    wT = nc.alloc_sbuf_tensor("wT", [128, DC * D], BF16)
    qt = nc.alloc_sbuf_tensor("qt", [128, DC * SL], BF16)
    sqv = nc.alloc_sbuf_tensor("sqv", [128, SL], BF16)
    q2r = nc.alloc_sbuf_tensor("q2r", [1, SL], F32)
    bk = [nc.alloc_psum_tensor(f"bk{i}", [128, 512], F32) for i in range(4)]
    E = em.emit

    load_fm(em, xsb, x2, DC, TOK, eng="gpsimd")

    for wsrc, dst, with_q2 in ((wqT, Qo, True), (wkT, Ko, False)):
        load_fm(em, wT, wsrc, DC, D, eng="gpsimd", tag="ldw")
        for ntile in range(NSLOT):
            for fc in range(DC):
                pb = bk[fc % 2]
                for dc in range(DC):
                    E("tensor", lambda e, fc=fc, nt=ntile, dc=dc, pb=pb: e.matmul(
                        pb[:, :], wT[:, dc * D + fc * 128: dc * D + fc * 128 + 128],
                        xsb[:, dc * TOK + nt * 512: dc * TOK + nt * 512 + 512],
                        start=(dc == 0), stop=(dc == DC - 1)),
                        reads=[wT, xsb], writes=[pb])
                E("scalar", lambda e, fc=fc, pb=pb: e.activation(
                    qt[:, fc * SL:(fc + 1) * SL], pb[:, :], AF.Copy),
                    reads=[pb], writes=[(qt, fc)])
            dstv = dst.rearrange("(c p) x -> p c x", p=128)[:, :, ntile * 512:(ntile + 1) * 512]
            E("sync", lambda e, dstv=dstv: e.dma_start(
                out=dstv, in_=qt[:, :].rearrange("p (c x) -> p c x", c=DC)),
                reads=[(qt, fc_) for fc_ in range(DC)], dma=True, tag="out")
            if with_q2:
                pq = bk[2]
                for fc in range(DC):
                    E("scalar", lambda e, fc=fc: e.activation(
                        sqv[:, :], qt[:, fc * SL:(fc + 1) * SL], AF.Square),
                        reads=[(qt, fc)], writes=[sqv])
                    E("tensor", lambda e, fc=fc, pq=pq: e.matmul(
                        pq[0:1, :], cm.ones_colb[:, :], sqv[:, :],
                        start=(fc == 0), stop=(fc == DC - 1)),
                        reads=[sqv, cm.ones_colb], writes=[pq])
                E("vector", lambda e, pq=pq: e.tensor_copy(q2r[0:1, :], pq[0:1, :]),
                  reads=[pq], writes=[q2r])
                E("sync", lambda e, nt=ntile: e.dma_start(
                    out=q2o[0:1, nt * 512:(nt + 1) * 512], in_=q2r[0:1, :]),
                    reads=[q2r], dma=True, tag="out")

    load_fm(em, wT, wvT, DC, D, eng="gpsimd", tag="ldw")
    for tc in range(TOK // 128):
        for half in range(2):
            pb = bk[half]
            for dc in range(DC):
                E("tensor", lambda e, tc=tc, half=half, dc=dc, pb=pb: e.matmul(
                    pb[:, :], xsb[:, dc * TOK + tc * 128: dc * TOK + tc * 128 + 128],
                    wT[:, dc * D + half * 512: dc * D + half * 512 + 512],
                    start=(dc == 0), stop=(dc == DC - 1)),
                    reads=[wT, xsb], writes=[pb])
            E("scalar", lambda e, half=half, pb=pb: e.activation(
                qt[:, half * 512: half * 512 + 512], pb[:, :], AF.Copy),
                reads=[pb], writes=[(qt, half)])
        E("sync", lambda e, tc=tc: e.dma_start(
            out=Vo[tc * 128:(tc + 1) * 128, :], in_=qt[:, 0:D]),
            reads=[(qt, 0), (qt, 1)], dma=True, tag="out")

    outs = [op for op in em.ops if op.tag == "out"]
    em.emit("sync", lambda e: e.nop(), after=outs, tag="final")
    em.build("C1")
    return nc


def build_C2():
    nc = _mk_nc()
    x2 = _in(nc, "x2", [D, TOK])
    KoI = _in(nc, "Ko", [D, TOK], BF16)
    QfI = _in(nc, "Qf", [D, S], BF16)
    VfI = _in(nc, "Vf", [S, D], BF16)
    q2f = _in(nc, "q2f", [1, S])
    cmask = _in(nc, "cmask", [4, 8, 128, 512], BF16)
    x3 = _out(nc, "x3", [D, TOK])

    em = Emitter(nc)
    cm = Common(nc, em)
    xrb = nc.alloc_sbuf_tensor("ln_xrb", [128, DC * SL], BF16)
    x2b_ = nc.alloc_sbuf_tensor("ln_x2b", [128, DC * SL], BF16)
    lrows = nc.alloc_sbuf_tensor("ln_rows", [1, 4 * SL], F32)
    cm.attach_ln_scratch(xrb, x2b_, lrows)

    Qf = nc.alloc_sbuf_tensor("Qf_sb", [128, DC * S], BF16)
    Ko = nc.alloc_sbuf_tensor("Ko_sb", [128, DC * TOK], BF16)
    Pp = nc.alloc_sbuf_tensor("Pp", [128, 32 * SL], BF16)
    vbuf = [nc.alloc_sbuf_tensor(f"vb{i}", [128, D], BF16) for i in range(2)]
    mbuf = [nc.alloc_sbuf_tensor(f"mb{i}", [128, SL], BF16) for i in range(2)]
    xot = nc.alloc_sbuf_tensor("xot", [128, DC * SL], F32)
    xres = nc.alloc_sbuf_tensor("xres", [128, DC * SL], F32)
    sqv = nc.alloc_sbuf_tensor("sqv", [128, SL], BF16)
    scr = nc.alloc_sbuf_tensor("scr", [1, SL], F32)
    sca = nc.alloc_sbuf_tensor("sca", [1, 16], F32)
    negc = nc.alloc_sbuf_tensor("negc", [128, 1], F32)
    rl = nc.alloc_sbuf_tensor("rl", [1, SL], F32)
    rlb = nc.alloc_sbuf_tensor("rlb", [128, SL], F32)
    bk = [nc.alloc_psum_tensor(f"bk{i}", [128, 512], F32) for i in range(8)]
    E = em.emit

    load_fm(em, Qf, QfI, DC, S, eng="sync")
    load_fm(em, Ko, KoI, DC, TOK, eng="sync")

    mk2, mq2 = sca[0:1, 0:1], sca[0:1, 1:2]
    prd, sqr = sca[0:1, 2:3], sca[0:1, 3:4]
    ngc, cur = sca[0:1, 4:5], sca[0:1, 5:6]
    E("vector", lambda e: e.memset(mk2, 0.0), writes=[(sca, "mk2")])
    E("vector", lambda e: e.memset(mq2, 0.0), writes=[(sca, "mq2")])
    for ntile in range(NSLOT):
        pq = bk[0]
        for fc in range(DC):
            E("scalar", lambda e, fc=fc, nt=ntile: e.activation(
                sqv[:, :], Ko[:, fc * TOK + nt * 512: fc * TOK + nt * 512 + 512], AF.Square),
                reads=[Ko], writes=[sqv])
            E("tensor", lambda e, fc=fc, pq=pq: e.matmul(
                pq[0:1, :], cm.ones_colb[:, :], sqv[:, :],
                start=(fc == 0), stop=(fc == DC - 1)),
                reads=[sqv, cm.ones_colb], writes=[pq])
        E("vector", lambda e, pq=pq: e.tensor_copy(scr[0:1, :], pq[0:1, :]),
          reads=[pq], writes=[scr])
        E("vector", lambda e: e.reduce_max(cur, scr[0:1, :], axis=AX.X),
          reads=[scr], writes=[(sca, "cur")])
        E("vector", lambda e: e.tensor_max(mk2, mk2, cur),
          reads=[(sca, "cur"), (sca, "mk2")], writes=[(sca, "mk2")])
    for ntile in range(S // 512):
        E("sync", lambda e, nt=ntile: e.dma_start(
            out=scr[0:1, :], in_=q2f[0:1, nt * 512:(nt + 1) * 512]),
            writes=[scr], dma=True)
        E("vector", lambda e: e.reduce_max(cur, scr[0:1, :], axis=AX.X),
          reads=[scr], writes=[(sca, "cur")])
        E("vector", lambda e: e.tensor_max(mq2, mq2, cur),
          reads=[(sca, "cur"), (sca, "mq2")], writes=[(sca, "mq2")])
    E("vector", lambda e: e.tensor_mul(prd, mk2, mq2),
      reads=[(sca, "mk2"), (sca, "mq2")], writes=[(sca, "prd")])
    E("scalar", lambda e: e.activation(sqr, prd, AF.Sqrt),
      reads=[(sca, "prd")], writes=[(sca, "sqr")])
    E("vector", lambda e: e.tensor_scalar(ngc, sqr, -1.0 / 32.0, None, ALU.mult),
      reads=[(sca, "sqr")], writes=[(sca, "ngc")])
    E("tensor", lambda e: e.matmul(bk[3][:, 0:1], cm.ones_row[:, :], ngc, start=True, stop=True),
      reads=[(sca, "ngc"), cm.ones_row], writes=[bk[3]])
    E("vector", lambda e: e.tensor_copy(negc[:, :], bk[3][:, 0:1]),
      reads=[bk[3]], writes=[negc])

    for it in range(NSLOT):
        for jc in range(32):
            sb_ = bk[jc % 2]
            for dc in range(DC):
                E("tensor", lambda e, jc=jc, it=it, dc=dc, sb_=sb_: e.matmul(
                    sb_[:, :], Qf[:, dc * S + jc * 128: dc * S + jc * 128 + 128],
                    Ko[:, dc * TOK + it * 512: dc * TOK + it * 512 + 512],
                    start=(dc == 0), stop=(dc == DC - 1)),
                    reads=[Qf, Ko], writes=[sb_])
            E("scalar", lambda e, jc=jc, sb_=sb_: e.activation(
                Pp[:, jc * SL:(jc + 1) * SL], sb_[:, :], AF.Exp,
                bias=negc[:, 0:1], scale=1.0 / 32.0),
                reads=[sb_, negc], writes=[(Pp, jc)])
            E("tensor", lambda e, jc=jc: e.matmul(
                bk[2][0:1, :], cm.ones_colb[:, :], Pp[:, jc * SL:(jc + 1) * SL],
                start=(jc == 0), stop=(jc == 31)),
                reads=[(Pp, jc), cm.ones_colb], writes=[bk[2]])
        for mi, g in enumerate(MSK_TILES[it]):
            for cl in range(4):
                jc = g * 4 + cl
                mb_ = mbuf[(mi * 4 + cl) % 2]
                E("sync", lambda e, it=it, mi=mi, cl=cl, mb_=mb_: e.dma_start(
                    out=mb_[:, :], in_=cmask[it, mi * 4 + cl]),
                    writes=[mb_], dma=True)
                E("vector", lambda e, jc=jc, mb_=mb_: e.tensor_mul(
                    Pp[:, jc * SL:(jc + 1) * SL], Pp[:, jc * SL:(jc + 1) * SL], mb_[:, :]),
                    reads=[(Pp, jc), mb_], writes=[(Pp, jc)])
        E("vector", lambda e: e.tensor_copy(scr[0:1, :], bk[2][0:1, :]),
          reads=[bk[2]], writes=[scr])
        E("vector", lambda e: e.reciprocal(rl[0:1, :], scr[0:1, :]),
          reads=[scr], writes=[rl])
        E("tensor", lambda e: e.matmul(bk[3][:, :], cm.ones_row[:, :], rl[0:1, :],
                                       start=True, stop=True),
          reads=[rl, cm.ones_row], writes=[bk[3]])
        E("vector", lambda e: e.tensor_copy(rlb[:, :], bk[3][:, :]),
          reads=[bk[3]], writes=[rlb])
        xsl = x2.rearrange("(c p) x -> p c x", p=128)[:, :, it * 512:(it + 1) * 512]
        E("sync", lambda e, xsl=xsl: e.dma_start(
            out=xot[:, :].rearrange("p (c x) -> p c x", c=DC), in_=xsl),
            writes=[xot], dma=True)
        chunks = [g * 4 + cl for g in INC_TILES[it] for cl in range(4)]
        for half in range(2):
            for ji, jc in enumerate(chunks):
                vb_ = vbuf[ji % 2]
                E("sync", lambda e, jc=jc, vb_=vb_: e.dma_start(
                    out=vb_[:, :], in_=VfI[jc * 128:(jc + 1) * 128, :]),
                    writes=[vb_], dma=True)
                for dvl in range(4):
                    dv = half * 4 + dvl
                    E("tensor", lambda e, jc=jc, dv=dv, dvl=dvl, vb_=vb_, ji=ji, n=len(chunks): e.matmul(
                        bk[4 + dvl][:, :], vb_[:, dv * 128:(dv + 1) * 128],
                        Pp[:, jc * SL:(jc + 1) * SL],
                        start=(ji == 0), stop=(ji == n - 1)),
                        reads=[vb_, (Pp, jc)], writes=[bk[4 + dvl]])
            for dvl in range(4):
                dv = half * 4 + dvl
                E("vector", lambda e, dvl=dvl: e.tensor_mul(
                    cm.tmp[:, :], bk[4 + dvl][:, :], rlb[:, :]),
                    reads=[bk[4 + dvl], rlb], writes=[cm.tmp])
                E("vector", lambda e, dv=dv: e.tensor_add(
                    xres[:, dv * SL:(dv + 1) * SL], cm.tmp[:, :], xot[:, dv * SL:(dv + 1) * SL]),
                    reads=[cm.tmp, xot], writes=[xres])
        cm.layernorm_nt(xres, bk[0], bk[1], out_f32=xres)
        dstv = x3.rearrange("(c p) x -> p c x", p=128)[:, :, it * 512:(it + 1) * 512]
        srcv = xres[:, :].rearrange("p (c x) -> p c x", c=DC)
        E("sync", lambda e, dstv=dstv, srcv=srcv: e.dma_start(out=dstv, in_=srcv),
          reads=[xres], dma=True, tag="out")

    outs = [op for op in em.ops if op.tag == "out"]
    em.emit("sync", lambda e: e.nop(), after=outs, tag="final")
    em.build("C2")
    return nc


def build_D():
    nc = _mk_nc()
    x3 = _in(nc, "x3", [D, TOK])
    WgT = _in(nc, "WgT", [D, 8])
    lnsI = _in(nc, "lns", [D, 8])
    lnbI = _in(nc, "lnb", [D, 8])
    W1f = _in(nc, "W1f", [D, 8 * 256])
    b1f = _in(nc, "b1f", [256, 8])
    W2f = _in(nc, "W2f", [256, 8 * D])
    b2I = _in(nc, "b2", [8, D])
    idI = _in(nc, "id128", [128, 128])
    indI = _in(nc, "ind", [8, 1024], BF16)
    xout = _out(nc, "xout", [D, TOK])

    em = Emitter(nc)
    cm = Common(nc, em)
    xrb = nc.alloc_sbuf_tensor("ln_xrb", [128, DC * SL], BF16)
    x2b_ = nc.alloc_sbuf_tensor("ln_x2b", [128, DC * SL], BF16)
    lrows = nc.alloc_sbuf_tensor("ln_rows", [1, 4 * SL], F32)
    cm.attach_ln_scratch(xrb, x2b_, lrows)

    w1e = nc.alloc_sbuf_tensor("w1e", [128, DC * 8 * 256], BF16)
    w2e = nc.alloc_sbuf_tensor("w2e", [128, 2 * 8 * D], BF16)
    wg = nc.alloc_sbuf_tensor("wg", [128, DC * 8], F32)
    lns = nc.alloc_sbuf_tensor("lns_sb", [128, DC * 8], F32)
    lnb = nc.alloc_sbuf_tensor("lnb_sb", [128, DC * 8], F32)
    b1s = nc.alloc_sbuf_tensor("b1s", [128, 2 * 8], F32)
    b2b = nc.alloc_sbuf_tensor("b2b", [8, D], BF16)
    idt = nc.alloc_sbuf_tensor("idt", [128, 128], F32)
    indb = nc.alloc_sbuf_tensor("indb", [8, 1024], BF16)
    xot = nc.alloc_sbuf_tensor("xot", [128, DC * SL], F32)
    xtb = nc.alloc_sbuf_tensor("xtb", [128, DC * SL], BF16)
    xhat = nc.alloc_sbuf_tensor("xhat", [128, DC * SL], BF16)
    xeb = nc.alloc_sbuf_tensor("xeb", [128, DC * SL], BF16)
    hall = nc.alloc_sbuf_tensor("hall", [128, 16 * SL], BF16)
    xres = nc.alloc_sbuf_tensor("xres", [128, DC * SL], F32)
    rmx = nc.alloc_sbuf_tensor("rmx", [128, 1], F32)
    mtm = nc.alloc_sbuf_tensor("mtm", [128, 8], F32)
    maskb = nc.alloc_sbuf_tensor("maskb", [8, SL], BF16)
    bk = [nc.alloc_psum_tensor(f"bk{i}", [128, 512], F32) for i in range(8)]
    E = em.emit

    load_fm(em, wg, WgT, DC, 8, eng="sync")
    load_fm(em, lns, lnsI, DC, 8, eng="sync")
    load_fm(em, lnb, lnbI, DC, 8, eng="sync")
    load_fm(em, w1e, W1f, DC, 8 * 256, eng="gpsimd")
    load_fm(em, b1s, b1f, 2, 8, eng="sync")
    load_fm(em, w2e, W2f, 2, 8 * D, eng="gpsimd")
    E("gpsimd", lambda e: e.dma_start(out=b2b[:, :], in_=b2I), writes=[b2b], dma=True)
    E("sync", lambda e: e.dma_start(out=idt[:, :], in_=idI), writes=[idt], dma=True)
    E("sync", lambda e: e.dma_start(out=indb[:, :], in_=indI), writes=[indb], dma=True)

    for tt in range(NSLOT):
        xsl = x3.rearrange("(c p) x -> p c x", p=128)[:, :, tt * 512:(tt + 1) * 512]
        E("sync", lambda e, xsl=xsl: e.dma_start(
            out=xot[:, :].rearrange("p (c x) -> p c x", c=DC), in_=xsl),
            writes=[xot], dma=True)
        E("gpsimd", lambda e, xsl=xsl: e.dma_start(
            out=xtb[:, :].rearrange("p (c x) -> p c x", c=DC), in_=xsl),
            writes=[xtb], dma=True)
        for m4 in range(4):
            pg = bk[0]
            for dc in range(DC):
                E("tensor", lambda e, m4=m4, dc=dc, pg=pg: e.matmul(
                    pg[:, 0:8], xot[:, dc * SL + m4 * 128: dc * SL + m4 * 128 + 128],
                    wg[:, dc * 8:(dc + 1) * 8],
                    start=(dc == 0), stop=(dc == DC - 1)),
                    reads=[xot, wg], writes=[pg])
            E("vector", lambda e, pg=pg: e.reduce_max(rmx[:, :], pg[:, 0:8], axis=AX.X),
              reads=[pg], writes=[rmx])
            E("vector", lambda e, pg=pg: e.tensor_scalar(
                mtm[:, :], pg[:, 0:8], rmx[:, 0:1], None, ALU.is_ge),
                reads=[pg, rmx], writes=[mtm])
            E("tensor", lambda e: e.transpose(bk[3][0:8, 0:128], mtm[:, :], idt[:, :]),
              reads=[mtm, idt], writes=[bk[3]])
            E("vector", lambda e, m4=m4: e.tensor_copy(
                maskb[0:8, m4 * 128:(m4 + 1) * 128], bk[3][0:8, 0:128]),
                reads=[bk[3]], writes=[maskb])
        cm.layernorm_nt(xot, bk[6], bk[7], out_f32=None, out_bf16=xhat)
        for ex in range(8):
            for dc in range(DC):
                E("vector", lambda e, ex=ex, dc=dc: e.tensor_scalar(
                    xeb[:, dc * SL:(dc + 1) * SL], xhat[:, dc * SL:(dc + 1) * SL],
                    lns[:, dc * 8 + ex: dc * 8 + ex + 1], lnb[:, dc * 8 + ex: dc * 8 + ex + 1],
                    ALU.mult, ALU.add),
                    reads=[xhat, lns, lnb], writes=[xeb])
            E("tensor", lambda e, ex=ex: e.matmul(
                bk[3][:, :], indb[0:8, ex * 128:(ex + 1) * 128], maskb[0:8, :],
                start=True, stop=True),
                reads=[indb, maskb], writes=[bk[3]])
            for hfc in range(2):
                ph = bk[1 + hfc]
                for dc in range(DC):
                    E("tensor", lambda e, ex=ex, hfc=hfc, dc=dc, ph=ph: e.matmul(
                        ph[:, :], w1e[:, dc * 2048 + ex * 256 + hfc * 128: dc * 2048 + ex * 256 + hfc * 128 + 128],
                        xeb[:, dc * SL:(dc + 1) * SL],
                        start=(dc == 0), stop=(dc == DC - 1)),
                        reads=[w1e, xeb], writes=[ph])
                E("scalar", lambda e, ex=ex, hfc=hfc, ph=ph: e.activation(
                    hall[:, (ex * 2 + hfc) * SL:(ex * 2 + hfc + 1) * SL], ph[:, :], AF.Gelu,
                    bias=b1s[:, hfc * 8 + ex: hfc * 8 + ex + 1]),
                    reads=[ph, b1s], writes=[(hall, (ex, hfc))])
                E("vector", lambda e, ex=ex, hfc=hfc: e.tensor_mul(
                    hall[:, (ex * 2 + hfc) * SL:(ex * 2 + hfc + 1) * SL],
                    hall[:, (ex * 2 + hfc) * SL:(ex * 2 + hfc + 1) * SL], bk[3][:, :]),
                    reads=[(hall, (ex, hfc)), bk[3]], writes=[(hall, (ex, hfc))])
        for fc in range(DC):
            py = bk[4 + fc % 2]
            for ex in range(8):
                for hfc in range(2):
                    E("tensor", lambda e, fc=fc, ex=ex, hfc=hfc, py=py: e.matmul(
                        py[:, :], w2e[:, hfc * 8 * D + ex * D + fc * 128: hfc * 8 * D + ex * D + fc * 128 + 128],
                        hall[:, (ex * 2 + hfc) * SL:(ex * 2 + hfc + 1) * SL],
                        start=(ex == 0 and hfc == 0), stop=False),
                        reads=[w2e, (hall, (ex, hfc))], writes=[py])
            E("tensor", lambda e, fc=fc, py=py: e.matmul(
                py[:, :], b2b[0:8, fc * 128:(fc + 1) * 128], maskb[0:8, :],
                start=False, stop=True),
                reads=[b2b, maskb], writes=[py])
            E("vector", lambda e, fc=fc, py=py: e.tensor_add(
                xres[:, fc * SL:(fc + 1) * SL], py[:, :], xot[:, fc * SL:(fc + 1) * SL]),
                reads=[py, xot], writes=[xres])
        cm.layernorm_nt(xres, bk[6], bk[7], out_f32=xres)
        dstv = xout.rearrange("(c p) x -> p c x", p=128)[:, :, tt * 512:(tt + 1) * 512]
        srcv = xres[:, :].rearrange("p (c x) -> p c x", c=DC)
        E("sync", lambda e, dstv=dstv, srcv=srcv: e.dma_start(out=dstv, in_=srcv),
          reads=[xres], dma=True, tag="out")

    outs = [op for op in em.ops if op.tag == "out"]
    em.emit("sync", lambda e: e.nop(), after=outs, tag="final")
    em.build("D")
    return nc


_NC_CACHE = {}


def _get_nc(name):
    if name not in _NC_CACHE:
        _NC_CACHE[name] = {"A": build_A, "B": build_B, "C1": build_C1,
                           "C2": build_C2, "D": build_D}[name]()
    return _NC_CACHE[name]


def _core_slices(x, c):
    b, par = c // 2, c % 2
    return np.concatenate([x[b, 512 * p:512 * p + 512, :] for p in POS[par]], 0)


def _build_cmask(par):
    m = np.zeros((4, 8, 128, 512), np.float32)
    tri = (np.arange(512)[None, :] >= np.arange(512)[:, None]).astype(np.float32)
    for it in range(4):
        p = POS[par][it]
        for mi, g in enumerate(MSK_TILES[it]):
            q = GPOS[g]
            if q < p:
                m[it, mi * 4:(mi + 1) * 4] = 1.0
            elif q == p:
                for cl in range(4):
                    m[it, mi * 4 + cl] = tri[cl * 128:(cl + 1) * 128, :]
    return m.astype(ml_dtypes.bfloat16)


def kernel(L0_attn_x, L0_attn_W_qkv_v, L0_attn_W_qkv_u, L0_attn_W_o_v,
           L0_attn_W_o_u, L0_ffn_w1, L0_ffn_b1, L0_ffn_w2, L0_ffn_b2,
           L1_attn_wq, L1_attn_wk, L1_attn_wv, L1_ffn_Wg, L1_ffn_ln_scale,
           L1_ffn_ln_bias, L1_ffn_W1, L1_ffn_b1, L1_ffn_W2, L1_ffn_b2):
    f32 = np.float32
    x = np.asarray(L0_attn_x, f32)
    cores = list(range(NCORE))

    WvT = np.ascontiguousarray(np.asarray(L0_attn_W_qkv_v, f32).T)
    WuT = np.ascontiguousarray(np.asarray(L0_attn_W_qkv_u, f32).T)
    WovT = np.ascontiguousarray(np.asarray(L0_attn_W_o_v, f32).T)
    WouT = np.ascontiguousarray(np.asarray(L0_attn_W_o_u, f32).T)
    in_maps = []
    for c in cores:
        own = _core_slices(x, c)
        peer = _core_slices(x, c ^ 1)
        xseq = np.ascontiguousarray(np.concatenate([own, peer], 0).T)
        in_maps.append({"xseq": xseq, "WvT": WvT, "WuT": WuT, "WovT": WovT, "WouT": WouT})
    rA = run_bass_kernel_spmd(_get_nc("A"), in_maps, core_ids=cores)
    x1c = [np.asarray(rA.results[c]["x1"]) for c in cores]

    w1T = np.ascontiguousarray(np.asarray(L0_ffn_w1, f32).T)
    w2T = np.ascontiguousarray(np.asarray(L0_ffn_w2, f32).T)
    b1c = np.ascontiguousarray(np.asarray(L0_ffn_b1, f32).reshape(32, 128).T)
    b2c = np.ascontiguousarray(np.asarray(L0_ffn_b2, f32).reshape(8, 128).T)
    in_maps = [{"x1": x1c[c], "w1T": w1T, "w2T": w2T, "b1c": b1c, "b2c": b2c}
               for c in cores]
    rB = run_bass_kernel_spmd(_get_nc("B"), in_maps, core_ids=cores)
    x2c = [np.asarray(rB.results[c]["x2"]) for c in cores]

    wqT = np.ascontiguousarray(np.asarray(L1_attn_wq, f32).T)
    wkT = np.ascontiguousarray(np.asarray(L1_attn_wk, f32).T)
    wvT = np.ascontiguousarray(np.asarray(L1_attn_wv, f32).T)
    in_maps = [{"x2": x2c[c], "wqT": wqT, "wkT": wkT, "wvT": wvT} for c in cores]
    rC1 = run_bass_kernel_spmd(_get_nc("C1"), in_maps, core_ids=cores)

    in_maps = []
    for c in cores:
        b, par = c // 2, c % 2
        e, o = 2 * b, 2 * b + 1
        Qf = np.concatenate([rC1.results[e]["Qo"], rC1.results[o]["Qo"]], axis=1)
        Vf = np.concatenate([rC1.results[e]["Vo"], rC1.results[o]["Vo"]], axis=0)
        q2f = np.concatenate([rC1.results[e]["q2o"], rC1.results[o]["q2o"]], axis=1)
        in_maps.append({"x2": x2c[c], "Ko": rC1.results[c]["Ko"], "Qf": Qf,
                        "Vf": Vf, "q2f": q2f, "cmask": _build_cmask(par)})
    rC2 = run_bass_kernel_spmd(_get_nc("C2"), in_maps, core_ids=cores)
    x3c = [np.asarray(rC2.results[c]["x3"]) for c in cores]

    Wg = np.asarray(L1_ffn_Wg, f32)
    W1 = np.asarray(L1_ffn_W1, f32)
    W2 = np.asarray(L1_ffn_W2, f32)
    ind = np.zeros((8, 1024), f32)
    for e in range(8):
        ind[e, e * 128:(e + 1) * 128] = 1.0
    shared = {
        "WgT": np.ascontiguousarray(Wg.T),
        "lns": np.ascontiguousarray(np.asarray(L1_ffn_ln_scale, f32).T),
        "lnb": np.ascontiguousarray(np.asarray(L1_ffn_ln_bias, f32).T),
        "W1f": np.ascontiguousarray(W1.transpose(1, 0, 2).reshape(1024, 8 * 256)),
        "b1f": np.ascontiguousarray(np.asarray(L1_ffn_b1, f32).T),
        "W2f": np.ascontiguousarray(W2.transpose(1, 0, 2).reshape(256, 8 * 1024)),
        "b2": np.ascontiguousarray(np.asarray(L1_ffn_b2, f32)),
        "id128": np.eye(128, dtype=f32),
        "ind": ind.astype(ml_dtypes.bfloat16),
    }
    in_maps = [{"x3": x3c[c], **shared} for c in cores]
    rD = run_bass_kernel_spmd(_get_nc("D"), in_maps, core_ids=cores)

    out = np.empty((B, S, D), f32)
    for c in cores:
        b, par = c // 2, c % 2
        got = np.asarray(rD.results[c]["xout"]).T
        for si, p in enumerate(POS[par]):
            out[b, 512 * p:512 * p + 512, :] = got[si * 512:(si + 1) * 512, :]
    return out
